# version 15
# baseline (speedup 1.0000x reference)
import concourse.bass_utils as _bu
if not getattr(_bu, "_ldw_patched", False):
    _orig_gwa = _bu.get_walrus_args
    def _gwa(*a, **k):
        return [x.replace("--enable-ldw-opt=false", "--enable-ldw-opt=true")
                for x in _orig_gwa(*a, **k)]
    _bu.get_walrus_args = _gwa
    _bu._ldw_patched = True
import numpy as np
import concourse.dve_ops as dve_ops
from concourse.dve_op import DveOp
from concourse.dve_spec import Spec, Src0, Src1, C0, C1, C2, C3, Zero, Bin, select
from concourse.dve_spec import lower as spec_lower
from concourse.dve_uop import AluOp
from concourse.dve_table_gen import dve_ver_for

_VER = dve_ver_for("TRN2")


def _f(x):
    return np.float64 if getattr(x, "dtype", None) == np.float64 else np.float32


def _mk(name, body, ref, perf_en=False):
    spec = Spec(body=body)
    sha = spec_lower(spec, ver=_VER, perf_en=perf_en, name=name).sha(_VER)
    return DveOp(name, spec, subdim=False, uops_sha={_VER: sha}, reference=ref,
                 perf_en={_VER: perf_en} if perf_en else {})


_t = Bin(AluOp.MULTIPLY, Src0, Src1)
QABS = _mk("QABS_ANT", Bin(AluOp.MAX, _t, Bin(AluOp.SUBTRACT, Zero, _t)),
           lambda in0, in1, s0, s1, imm2: np.abs(in0.astype(_f(in0)) * in1))

_lo = (Bin(AluOp.IS_GT, Src0, C0) + Bin(AluOp.IS_GT, Src0, C1)) + \
      (Bin(AluOp.IS_GT, Src0, C2) + Bin(AluOp.IS_GT, Src0, C3))


def _qlo_ref(in0, in1, s0, s1, imm2):
    f = _f(in0)
    a = in0.astype(f)
    c3 = np.asarray(in1, f).reshape(-1, 1)
    while c3.ndim < a.ndim:
        c3 = c3[..., None]
    return ((a > s0).astype(f) + (a > s1) + (a > imm2) + (a > c3)).astype(f)


QLO = _mk("QLO_ANT", _lo, _qlo_ref)

_c6 = Bin(AluOp.IS_GT, Src0, C2)
_hi = ((Bin(AluOp.IS_GT, Src0, C0) + Bin(AluOp.IS_GT, Src0, C1)) + _c6) + _c6


def _qhi_ref(in0, in1, s0, s1, imm2):
    f = _f(in0)
    a = in0.astype(f)
    return ((a > s0).astype(f) + (a > s1) + 2.0 * (a > imm2)).astype(f)


QHI = _mk("QHI_ANT", _hi, _qhi_ref)

_cmb = (Src0 * C0 + C1) + Src1


def _qcmb_ref(in0, in1, s0, s1, imm2):
    f = _f(in0)
    return (in0.astype(f) * s0 + s1) + in1


QCMB = _mk("QCMB_ANT", _cmb, _qcmb_ref, perf_en=True)

_sg = select(Bin(AluOp.IS_LT, Src1, Zero), Bin(AluOp.SUBTRACT, Zero, Src0), Src0)


def _qsgn_ref(in0, in1, s0, s1, imm2):
    f = _f(in0)
    a = in0.astype(f)
    return np.where(in1 < 0, -a, a).astype(f)


QSGN = _mk("QSGN_ANT", _sg, _qsgn_ref)

_VICTIMS = ("DVEHASH32L_SEED", "DVEHASH32L_BODY", "DVEHASH32L_DRAIN",
            "BITWISE_XOR_TT_BV", "SHIFT_RIGHT_TT_BV")
_NEW = (QABS, QLO, QHI, QCMB, QSGN)


def register():
    if QABS.name in dve_ops._SUB_OPCODE_FOR_NAME:
        return
    for victim, op in zip(_VICTIMS, _NEW):
        idx = next(i for i, o in enumerate(dve_ops.OPS) if o.name == victim)
        row = dve_ops._SUB_OPCODE_FOR_NAME.pop(victim)
        dve_ops.OPS[idx] = op
        dve_ops._SUB_OPCODE_FOR_NAME[op.name] = row
        dve_ops.CUSTOM_DVE_SPECS[op.name] = op.spec
        dve_ops.CUSTOM_DVE_REFS[op.name] = op.reference


import numpy as np
from contextlib import ExitStack
import concourse.bass as bass
import concourse.mybir as mybir

F32 = mybir.dt.float32
BF16 = mybir.dt.bfloat16
ALU = mybir.AluOpType
ACT = mybir.ActivationFunctionType
AX = mybir.AxisListType

S = 4096
D = 1024
FI = 2816
NT = S // 128
NKD = D // 128
NKI = FI // 128
MIDS = (0.25, 0.75, 1.25, 1.75, 2.5, 3.5, 5.0)
DELTAS = (0.5, 0.5, 0.5, 0.5, 1.0, 1.0, 2.0)
QCHUNK = 704

ENGINES = ("sync", "gpsimd", "vector", "scalar", "tensor")

_register_qops = register


class Sem:
    __slots__ = ("h", "c")
    def __init__(self, h):
        self.h = h
        self.c = 0
    def inc(self, n=1):
        self.c += n
        return self.c


class KB:
    def __init__(self, dev_outputs=False):
        self.dev = dev_outputs
        self.nc = bass.Bass("TRN2", target_bir_lowering=False, debug=False,
                            detect_race_conditions=False)
        self.es = ExitStack()
        self.ops = {e: [] for e in ENGINES}
        self._uid = 0
        self._dmac = {}
        self._sect = {}

    def uid(self):
        self._uid += 1
        return str(self._uid)

    def E(self, eng, fn):
        self.ops[eng].append(fn)

    def dma(self, eng, sem, out, in_, **kw):
        wv = self._dmac.get(id(sem), 0)
        self._dmac[id(sem)] = wv + 16
        if wv:
            eng.wait_ge(sem.h, wv)
        eng.dma_start(out=out, in_=in_, **kw).then_inc(sem.h, 16)

    def vgap(self, v):
        v.tensor_scalar(out=self.vgapb, in0=self.vgapb, scalar1=1.0,
                        scalar2=None, op0=ALU.mult)

    def tap(self, name, ap, waits=()):
        if not self.dev:
            return None
        t = self.nc.dram_tensor("tap_" + name, list(ap.shape), ap.dtype,
                                kind="ExternalOutput").ap()
        def g(gp, t=t, ap=ap, waits=tuple(waits)):
            for h, c in waits:
                gp.wait_ge(h, c)
            self.dma(gp, self.sgp, t, ap)
        self.E("gpsimd", g)
        return self.sgp.inc(16)

    def flush(self):
        for eng in ENGINES:
            lst = self.ops[eng]
            if not lst:
                continue
            head = None
            tail = None
            if eng == "vector":
                prev = self._sect.get(eng)
                if prev is not None:
                    head = lambda v, c=prev: v.wait_ge(self.sv.h, c)
                tail = lambda v: v.engine_nop().then_inc(self.sv.h, 1)
                self._sect[eng] = self.sv.inc()
            elif eng == "scalar":
                prev = self._sect.get(eng)
                if prev is not None:
                    head = lambda s, c=prev: s.wait_ge(self.ss.h, c)
                tail = lambda s: s.copy(out=self.sgapb,
                                        in_=self.epsv).then_inc(self.ss.h, 1)
                self._sect[eng] = self.ss.inc()
            elif eng == "tensor":
                prev = self._sect.get(eng)
                if prev is not None:
                    head = lambda t, c=prev: t.wait_ge(self.st.h, c)
                sev = self.ss.c
                def tail(t, sev=sev):
                    t.wait_ge(self.ss.h, sev)
                    t.transpose(self.psT[:, 0:128], self.ident,
                                self.ident).then_inc(self.st.h, 1)
                self._sect[eng] = self.st.inc()
            def runner(e, lst=lst, head=head, tail=tail):
                if head is not None:
                    head(e)
                for f in lst:
                    f(e)
                if tail is not None:
                    tail(e)
            getattr(self.block, eng)(runner)
        self.ops = {e: [] for e in ENGINES}

    def barrier(self):
        self.flush()
        snaps = [(s.h, s.c) for s in (self.sin, self.sgp, self.sv, self.ss, self.st) if s.c > 0]
        for eng in ENGINES:
            def w(e, snaps=tuple(snaps)):
                for h, c in snaps:
                    e.wait_ge(h, c)
            self.E(eng, w)

    def build(self, phases=("P", "A", "B1", "B2", "C")):
        _register_qops()
        nc = self.nc
        T = lambda n, shp, dt, kind: nc.dram_tensor(n, shp, dt, kind=kind).ap()
        self.x_in = T("x_in", [S, D], F32, "ExternalInput")
        self.wg_in = T("wg_in", [FI, D], F32, "ExternalInput")
        self.wu_in = T("wu_in", [FI, D], F32, "ExternalInput")
        self.wd_in = T("wd_in", [D, FI], F32, "ExternalInput")
        self.Wq_in = T("Wq_in", [D, D], F32, "ExternalInput")
        self.Wk_in = T("Wk_in", [D, D], F32, "ExternalInput")
        self.Wv_in = T("Wv_in", [D, D], F32, "ExternalInput")
        self.Wo_in = T("Wo_in", [D, D], F32, "ExternalInput")
        self.bq_in = T("bq_in", [D], F32, "ExternalInput")
        self.bk_in = T("bk_in", [D], F32, "ExternalInput")
        self.bv_in = T("bv_in", [D], F32, "ExternalInput")
        self.bo_in = T("bo_in", [D], F32, "ExternalInput")
        self.ident_in = T("ident_in", [128, 128], F32, "ExternalInput")
        self.causal_in = T("causal_in", [128, 128], F32, "ExternalInput")
        self.ones_in = T("ones_in", [1, 128], F32, "ExternalInput")
        sk = "ExternalOutput" if self.dev else "Internal"
        self.x1_hbm = T("x1_hbm", [S, D], F32, sk)
        self.x2_hbm = T("x2_hbm", [S, D], F32, sk)
        self.wgT_hbm = T("wgT_hbm", [128, NKD * FI], BF16, "Internal")
        self.wuT_hbm = T("wuT_hbm", [128, NKD * FI], BF16, "Internal")
        self.wdT_hbm = T("wdT_hbm", [128, NKI * D], BF16, "Internal")
        self.out = T("out", [S, D], F32, "ExternalOutput")

        es = self.es
        sb = lambda n, shp, dt: es.enter_context(nc.sbuf_tensor(n, shp, dt))[:]
        psf = lambda n, shp, dt: es.enter_context(nc.psum_tensor(n, shp, dt))[:]
        sem = lambda n: Sem(es.enter_context(nc.semaphore(n)))

        self.sin = sem("sin")
        self.sgp = sem("sgp")
        self.sv = sem("sv")
        self.ss = sem("ss")
        self.st = sem("st")

        self.ident = sb("ident", [128, 128], BF16)
        self.causal = sb("causal", [128, 128], F32)
        self.onesb = sb("onesb", [1, 128], BF16)
        self.bvb = sb("bvb", [1, D], BF16)
        self.bob = sb("bob", [1, D], BF16)
        self.bkp = sb("bkp", [128, NKD], F32)
        self.bqp32 = sb("bqp32", [128, NKD], F32)
        self.xt = [sb("xt0", [128, D], F32), sb("xt1", [128, D], F32)]
        self.lnb = {k: sb("ln_" + k, [128, 1], F32)
                    for k in ("sm", "m", "ssum", "msq", "var", "sqv", "rstd")}
        self.epsv = sb("epsv", [128, 1], F32)
        self.c175 = sb("c175", [128, 1], F32)
        self.vgapb = sb("vgapb", [128, 32], F32)
        self.sgapb = sb("sgapb", [128, 1], F32)
        self.psT = psf("psT", [128, 8 * 128], BF16)
        self.psg = [psf("psg0", [128, 512], F32), psf("psg1", [128, 512], F32)]
        self.psu = [psf("psu0", [128, 512], F32), psf("psu1", [128, 512], F32)]
        self.psd = [psf("psd0", [128, 512], F32), psf("psd1", [128, 512], F32)]

        self.block = es.enter_context(nc.Block())
        sin, sgp, sv = self.sin, self.sgp, self.sv

        def g0(g):
            self.dma(g, sgp, self.ident, self.ident_in)
            self.dma(g, sgp, self.onesb, self.ones_in)
            self.dma(g, sgp, self.bvb, self.bv_in.unsqueeze(0))
            self.dma(g, sgp, self.bob, self.bo_in.unsqueeze(0))
        self.E("gpsimd", g0)
        sgp.inc(64)
        self._ident_ready = 16
        def s0(sy):
            self.dma(sy, sin, self.causal, self.causal_in)
            self.dma(sy, sin, self.bkp, self.bk_in.rearrange("(c p) -> p c", p=128),
                         allow_slow_non_contiguous=True)
            self.dma(sy, sin, self.bqp32, self.bq_in.rearrange("(c p) -> p c", p=128),
                         allow_slow_non_contiguous=True)
        self.E("sync", s0)
        sin.inc(48)
        t_bq = sin.c
        def v0(v, t=t_bq):
            v.wait_ge(sin.h, t)
            z1 = self.bqp32[:, 0:1]
            v.tensor_scalar(out=self.epsv, in0=z1, scalar1=0.0, scalar2=1e-5,
                            op0=ALU.mult, op1=ALU.add)
            v.tensor_scalar(out=self.c175, in0=z1, scalar1=0.0, scalar2=1.75,
                            op0=ALU.mult, op1=ALU.add)
            v.tensor_scalar(out=self.vgapb, in0=z1.broadcast_to([128, 32]),
                            scalar1=0.0, scalar2=0.0, op0=ALU.mult, op1=ALU.add)
            v.tensor_scalar(out=self.bqp32, in0=self.bqp32, scalar1=1.0 / 32.0,
                            scalar2=None, op0=ALU.mult).then_inc(sv.h, 1)
        self.E("vector", v0)
        sv.inc()

        if "P" in phases or "A" in phases:
            with ExitStack() as pes:
                self.ffn_wbufs(pes)
                if "P" in phases:
                    with ExitStack() as pro_es:
                        u2 = self.uid()
                        self.wstage = pro_es.enter_context(
                            nc.sbuf_tensor("wstage" + u2, [128, FI], F32))[:]
                        self.wqnat = pro_es.enter_context(
                            nc.sbuf_tensor("wqnat" + u2, [128, FI], BF16))[:]
                        self.phase_prologue()
                        self.barrier()
                if "A" in phases:
                    self.ffn_tilebufs(pes)
                    self.phase_ffn(self.x_in, self.x1_hbm)
                    self.barrier()
        if "B1" in phases or "B2" in phases:
            with ExitStack() as pes:
                sbp = lambda n, shp, dt: pes.enter_context(nc.sbuf_tensor(n, shp, dt))[:]
                self.k_sb = sbp("k_sb", [128, NKD * S], BF16)
                self.v_sb = sbp("v_sb", [128, NT * D], BF16)
                self.xln_bf = sbp("xln_bf", [128, D], BF16)
                self.xlnT = sbp("xlnT", [128, NKD * 128], BF16)
                if "B1" in phases:
                    with ExitStack() as b1es:
                        sb1 = lambda n, shp, dt: b1es.enter_context(nc.sbuf_tensor(n, shp, dt))[:]
                        self.Wk_sb = sb1("Wk_sb", [128, NKD * D], BF16)
                        self.Wv_sb = sb1("Wv_sb", [128, NKD * D], BF16)
                        self.lnsq = sb1("lnsq_b", [128, D], F32)
                        self.phase_b1()
                        self.barrier()
                if "B2" in phases:
                    with ExitStack() as b2es:
                        sb2 = lambda n, shp, dt: b2es.enter_context(nc.sbuf_tensor(n, shp, dt))[:]
                        self.Wq_sb = sb2("Wq_sb", [128, NKD * D], BF16)
                        self.Wo_sb = sb2("Wo_sb", [128, NKD * D], BF16)
                        self.scores = sb2("scores", [128, S], F32)
                        self.attn_bf = sb2("attn_bf", [128, S], BF16)
                        self.q_fm = sb2("q_fm", [128, NKD * 128], BF16)

                        self.rowsum = sb2("rowsum", [128, 1], F32)
                        self.negmax = sb2("negmax", [128, 1], F32)
                        self.rcpsum = sb2("rcpsum", [128, 1], F32)
                        self.attnT = sb2("attnT", [128, 8 * 128], BF16)
                        self.y_bf = self.attn_bf[:, 0:D]
                        self.yT = self.attn_bf[:, D:2 * D]
                        self.phase_b2()
                        self.barrier()
        if "C" in phases:
            with ExitStack() as pes:
                self.ffn_wbufs(pes)
                self.ffn_tilebufs(pes)
                def ldw(sy):
                    self.dma(sy, self.sin, self.wgT, self.wgT_hbm)
                    self.dma(sy, self.sin, self.wuT, self.wuT_hbm)
                    self.dma(sy, self.sin, self.wdT, self.wdT_hbm)
                self.E("sync", ldw)
                self.sin.inc(48)
                wq_ready = self.sin.c
                self.phase_ffn(self.x2_hbm, self.out, wq_wait=wq_ready)
                self.flush()
        def fin(g):
            g.wait_ge(self.sgp.h, self.sgp.c)
        self.E("gpsimd", fin)
        self.flush()
        self.es.close()
        return nc

    def ffn_wbufs(self, pes):
        nc = self.nc
        u = self.uid()
        sbp = lambda n, shp, dt: pes.enter_context(nc.sbuf_tensor(n + u, shp, dt))[:]
        self.wgT = sbp("wgT", [128, NKD * FI], BF16)
        self.wuT = sbp("wuT", [128, NKD * FI], BF16)
        self.wdT = sbp("wdT", [128, NKI * D], BF16)
        self.q = {
            "absz": sbp("q_absz", [128, QCHUNK], F32),
            "an": sbp("q_an", [128, QCHUNK], F32),
            "ca": sbp("q_ca", [128, QCHUNK], BF16),
            "cb": sbp("q_cb", [128, QCHUNK], BF16),
            "gm": sbp("q_gm", [128, QCHUNK // 16], F32),
            "sc": sbp("q_sc", [128, QCHUNK // 16], F32),
            "rcp": sbp("q_rcp", [128, QCHUNK // 16], F32),
            "sg16": sbp("q_sg16", [128, QCHUNK], BF16),
        }

    def ffn_tilebufs(self, pes):
        nc = self.nc
        u = self.uid()
        sbp = lambda n, shp, dt: pes.enter_context(nc.sbuf_tensor(n + u, shp, dt))[:]
        self.xln = sbp("xln", [128, D], F32)
        self.xq = [sbp("xq0", [128, D], BF16), sbp("xq1", [128, D], BF16)]
        self.xt3 = [self.xt[0], self.xt[1], sbp("xt2", [128, D], F32),
                    sbp("xt3", [128, D], F32)]
        self.xqT = [sbp("xqT0", [128, NKD * 128], BF16),
                    sbp("xqT1", [128, NKD * 128], BF16),
                    sbp("xqT2", [128, NKD * 128], BF16)]
        self.sg_sb = sbp("sg_sb", [128, FI], F32)
        self.hq = sbp("hq", [128, FI], BF16)
        self.hqT = sbp("hqT", [128, NKI * 128], BF16)
        self.x1t = sbp("x1t", [128, D], F32)

    def emit_quant(self, z_ap, out_ap, width, pre_waits=()):
        q = self.q
        nchunks = (width + QCHUNK - 1) // QCHUNK
        for c in range(nchunks):
            w = min(QCHUNK, width - c * QCHUNK)
            g = w // 16
            z = z_ap[:, c * QCHUNK:c * QCHUNK + w]
            o = out_ap[:, c * QCHUNK:c * QCHUNK + w]
            pw = tuple(pre_waits) if c == 0 else ()
            def f(v, z=z, o=o, w=w, g=g, pw=pw):
                for h, cnt in pw:
                    v.wait_ge(h, cnt)
                an = q["an"][:, :w]; qa = q["absz"][:, :w]
                ca = q["ca"][:, :w]; cb = q["cb"][:, :w]; sg16 = q["sg16"][:, :w]
                gm = q["gm"][:, :g]; sc = q["sc"][:, :g]; rcp = q["rcp"][:, :g]
                v.tensor_reduce(out=gm, in_=z.rearrange("p (g k) -> p g k", k=16),
                                axis=AX.X, op=ALU.max, apply_absolute_value=True)
                self.vgap(v)
                v.tensor_scalar(out=sc, in0=gm, scalar1=1.0 / 6.0, scalar2=1e-12,
                                op0=ALU.mult, op1=ALU.max)
                self.vgap(v)
                v.reciprocal_approx_fast(out=rcp, in_=sc)
                self.vgap(v)
                rb = rcp.unsqueeze(2).broadcast_to([128, g, 16])
                v._custom_dve(QABS, out=an.rearrange("p (g k) -> p g k", k=16),
                              in0=z.rearrange("p (g k) -> p g k", k=16), in1=rb)
                v._custom_dve(QLO, out=ca, in0=an, in1=self.c175,
                              s0=MIDS[0], s1=MIDS[1], imm2=MIDS[2])
                v._custom_dve(QHI, out=cb, in0=an,
                              s0=MIDS[4], s1=MIDS[5], imm2=MIDS[6])
                v._custom_dve(QCMB, out=sg16, in0=ca, in1=cb, s0=0.5, s1=0.0)
                sbc = sc.unsqueeze(2).broadcast_to([128, g, 16])
                v.tensor_tensor(out=qa.rearrange("p (g k) -> p g k", k=16),
                                in0=sg16.rearrange("p (g k) -> p g k", k=16),
                                in1=sbc, op=ALU.mult)
                v._custom_dve(QSGN, out=o, in0=qa, in1=z)
            self.E("vector", f)

    def emit_ln(self, xt, xln_out, s_extra_waits=(), sqscr=None):
        lnb = self.lnb
        sv, ss = self.sv, self.ss
        if sqscr is None:
            sqscr = xln_out
        def v1(v, sqscr=sqscr):
            v.reduce_sum(out=lnb["sm"], in_=xt, axis=AX.X)
            v.scalar_tensor_tensor(out=sqscr, in0=xt, scalar=1.0, in1=xt,
                                   op0=ALU.mult, op1=ALU.mult,
                                   accum_out=lnb["ssum"])
            v.tensor_scalar(out=lnb["m"], in0=lnb["sm"], scalar1=1.0 / D,
                            scalar2=None, op0=ALU.mult)
            v.tensor_scalar(out=lnb["var"], in0=lnb["ssum"], scalar1=1.0 / D,
                            scalar2=None, op0=ALU.mult)
            v.tensor_tensor(out=lnb["msq"], in0=lnb["m"], in1=lnb["m"], op=ALU.mult)
            self.vgap(v)
            self.vgap(v)
            v.tensor_tensor(out=lnb["var"], in0=lnb["var"], in1=lnb["msq"],
                            op=ALU.subtract).then_inc(sv.h, 1)
        self.E("vector", v1)
        t_var = sv.inc()
        def s2(s, t=t_var, sw=tuple(s_extra_waits)):
            for h, c in sw:
                s.wait_ge(h, c)
            s.wait_ge(sv.h, t)
            s.activation(out=lnb["sqv"], in_=lnb["var"], func=ACT.Sqrt,
                         bias=self.epsv).then_inc(ss.h, 1)
        self.E("scalar", s2)
        t_sqv = ss.inc()
        def v3(v, t=t_sqv):
            v.wait_ge(ss.h, t)
            v.reciprocal(out=lnb["rstd"], in_=lnb["sqv"])
            self.vgap(v)
            self.vgap(v)
            v.tensor_scalar(out=xln_out, in0=xt, scalar1=lnb["m"],
                            scalar2=lnb["rstd"], op0=ALU.subtract, op1=ALU.mult)
        self.E("vector", v3)

    def emit_transpose_batch(self, srcs, dst_ap, twaits=(), swaits=()):
        st, ss = self.st, self.ss
        nb = len(srcs)
        if self._ident_ready is not None:
            twaits = list(twaits) + [(self.sgp.h, self._ident_ready)]
            self._ident_ready = None
        def t1(t, srcs=tuple(srcs), twaits=tuple(twaits)):
            for h, c in twaits:
                t.wait_ge(h, c)
            last = None
            for j, s in enumerate(srcs):
                last = t.transpose(self.psT[:, j * 128:(j + 1) * 128], s, self.ident)
            last.then_inc(st.h, 1)
        self.E("tensor", t1)
        t_tp = st.inc()
        def s1(s, dst_ap=dst_ap, nb=nb, t=t_tp, swaits=tuple(swaits)):
            s.wait_ge(st.h, t)
            for h, c in swaits:
                s.wait_ge(h, c)
            src = self.psT[:, :nb * 128]
            if dst_ap.ndim == 3:
                src = src.rearrange("p (j f) -> p j f", f=128)
            s.copy(out=dst_ap, in_=src).then_inc(ss.h, 1)
        self.E("scalar", s1)
        return ss.inc()

    def phase_prologue(self):
        sin, sv, st, ss, sgp = self.sin, self.sv, self.st, self.ss, self.sgp
        jobs = [(self.wg_in, NKI, NKD, FI, self.wgT, self.wgT_hbm),
                (self.wu_in, NKI, NKD, FI, self.wuT, self.wuT_hbm),
                (self.wd_in, NKD, NKI, D, self.wdT, self.wdT_hbm)]
        last_tp = 0
        quant_done = 0
        for (win, nchunks, nkc, tcols, wT, wT_hbm) in jobs:
            rowlen = win.shape[1]
            for nch in range(nchunks):
                def ld(sy, win=win, nch=nch, rowlen=rowlen, tw=quant_done):
                    if tw:
                        sy.wait_ge(sv.h, tw)
                    self.dma(sy, sin, self.wstage[:, :rowlen], win[nch * 128:(nch + 1) * 128, :])
                self.E("sync", ld)
                t_ld = sin.inc(16)
                self.emit_quant(self.wstage[:, :rowlen], self.wqnat[:, :rowlen], rowlen,
                                pre_waits=[(sin.h, t_ld), (st.h, last_tp)])
                def vq(v):
                    v.engine_nop().then_inc(sv.h, 1)
                self.E("vector", vq)
                quant_done = sv.inc()
                wt3 = wT.rearrange("p (kc f) -> p kc f", f=tcols)
                for b0 in range(0, nkc, 8):
                    nb = min(8, nkc - b0)
                    srcs = [self.wqnat[:, (b0 + j) * 128:(b0 + j + 1) * 128]
                            for j in range(nb)]
                    dst = wt3[:, b0:b0 + nb, nch * 128:(nch + 1) * 128]
                    self.emit_transpose_batch(srcs, dst,
                                              twaits=[(sv.h, quant_done), (ss.h, ss.c)])
                last_tp = st.c
            def sp(g, wT=wT, wT_hbm=wT_hbm, t=ss.c):
                g.wait_ge(ss.h, t)
                self.dma(g, sgp, wT_hbm, wT)
            self.E("gpsimd", sp)
            sgp.inc(16)

    def phase_ffn(self, src_hbm, dst_hbm, wq_wait=None):
        sin, sv, st, ss, sgp = self.sin, self.sv, self.st, self.ss, self.sgp
        resid_done = {}
        out_done = {}
        t_xqT = {}
        pend_tp = {}

        def emit_xq_tp(j):
            if j not in pend_tp or j in t_xqT:
                return
            srcs = [self.xq[j % 2][:, c * 128:(c + 1) * 128] for c in range(NKD)]
            tws = [(sv.h, pend_tp[j]), (ss.h, ss.c)]
            t_xqT[j] = self.emit_transpose_batch(srcs, self.xqT[j % 3], twaits=tws)
        silu_hist = {}
        hmul_hist = {}
        NJG = (FI + 511) // 512

        def stage1(i):
            b = i % 2
            bx = i % 4
            tw = resid_done.get(i - 4)
            def ld(sy, i=i, bx=bx, tw=tw):
                if tw is not None:
                    sy.wait_ge(sv.h, tw)
                self.dma(sy, sin, self.xt3[bx], src_hbm[i * 128:(i + 1) * 128, :])
            self.E("sync", ld)
            t_x = sin.inc(16)
            def vw(v, t=t_x, txe=t_xqT.get(i - 2)):
                v.wait_ge(sin.h, t)
                if txe is not None:
                    v.wait_ge(ss.h, txe)
            self.E("vector", vw)
            self.emit_ln(self.xt3[bx], self.xln)
            self.emit_quant(self.xln, self.xq[b], D)
            def vq(v):
                v.engine_nop().then_inc(sv.h, 1)
            self.E("vector", vq)
            pend_tp[i] = sv.inc()


        def stage2(i, mid_hook=None):
            b = i % 2
            bx = i % 4
            silu_done = {}
            hmul_done = {}
            hq_done = {}
            sp = silu_hist.get(i - 1, {})
            hp = hmul_hist.get(i - 1, {})

            def emit_gu(jn):
                w = min(512, FI - jn * 512)
                pb = jn % 2
                tws = silu_done.get(jn - 2, sp.get(jn + NJG - 2))
                twh = hmul_done.get(jn - 2, hp.get(jn + NJG - 2))
                def tgu(t, jn=jn, w=w, pb=pb, i=i, b=b, tws=tws, twh=twh):
                    t.wait_ge(ss.h, t_xqT[i])
                    if i == 0 and jn == 0 and wq_wait is not None:
                        t.wait_ge(sin.h, wq_wait)
                    if tws is not None:
                        t.wait_ge(ss.h, tws)
                    if twh is not None:
                        t.wait_ge(sv.h, twh)
                    last = None
                    for kc in range(NKD):
                        t.matmul(self.psg[pb][:, :w],
                                 self.xqT[i % 3][:, kc * 128:(kc + 1) * 128],
                                 self.wgT[:, kc * FI + jn * 512: kc * FI + jn * 512 + w],
                                 start=(kc == 0), stop=(kc == NKD - 1))
                        last = t.matmul(self.psu[pb][:, :w],
                                        self.xqT[i % 3][:, kc * 128:(kc + 1) * 128],
                                        self.wuT[:, kc * FI + jn * 512: kc * FI + jn * 512 + w],
                                        start=(kc == 0), stop=(kc == NKD - 1))
                    last.then_inc(st.h, 1)
                self.E("tensor", tgu)
                t_gu = st.inc()
                def se(s, jn=jn, w=w, pb=pb, t=t_gu):
                    s.wait_ge(st.h, t)
                    s.activation(out=self.sg_sb[:, jn * 512:jn * 512 + w],
                                 in_=self.psg[pb][:, :w], func=ACT.Silu).then_inc(ss.h, 1)
                self.E("scalar", se)
                silu_done[jn] = ss.inc()
                def vh(v, jn=jn, w=w, pb=pb, t=t_gu, tsil=silu_done[jn]):
                    v.wait_ge(st.h, t)
                    v.wait_ge(ss.h, tsil)
                    sl = slice(jn * 512, jn * 512 + w)
                    v.tensor_tensor(out=self.sg_sb[:, sl], in0=self.sg_sb[:, sl],
                                    in1=self.psu[pb][:, :w], op=ALU.mult).then_inc(sv.h, 1)
                self.E("vector", vh)
                hmul_done[jn] = sv.inc()
                self.emit_quant(self.sg_sb[:, jn * 512:jn * 512 + w],
                                self.hq[:, jn * 512:jn * 512 + w], w)
                def vq(v):
                    v.engine_nop().then_inc(sv.h, 1)
                self.E("vector", vq)
                hq_done[jn] = sv.inc()

            def emit_dn(jn):
                w = min(512, FI - jn * 512)
                nb = w // 128
                c0 = jn * 4
                srcs = [self.hq[:, (c0 + j) * 128:(c0 + j + 1) * 128] for j in range(nb)]
                dst = self.hqT[:, c0 * 128:(c0 + nb) * 128]
                t_hqT = self.emit_transpose_batch(srcs, dst,
                                                  twaits=[(sv.h, hq_done[jn]),
                                                          (ss.h, ss.c)])
                def td(t, jn=jn, nb=nb, c0=c0, t_hqT=t_hqT,
                       twp=(resid_done.get(i - 1) if jn == 0 else None)):
                    t.wait_ge(ss.h, t_hqT)
                    if twp is not None:
                        t.wait_ge(sv.h, twp)
                    last = None
                    for kc in range(c0, c0 + nb):
                        for j2 in range(2):
                            last = t.matmul(self.psd[j2][:, :512],
                                            self.hqT[:, kc * 128:(kc + 1) * 128],
                                            self.wdT[:, kc * D + j2 * 512: kc * D + j2 * 512 + 512],
                                            start=(kc == 0),
                                            stop=(kc == NKI - 1))
                    if jn == NJG - 1:
                        last.then_inc(st.h, 1)
                self.E("tensor", td)
                if jn == NJG - 1:
                    return st.inc()
                return None

            t_d = None
            for jn in range(NJG):
                emit_gu(jn)
                if jn == 1 and mid_hook is not None:
                    mid_hook()
                if jn == 3:
                    emit_xq_tp(i + 2)
                if jn >= 1:
                    t_d = emit_dn(jn - 1)
            t_d = emit_dn(NJG - 1)
            silu_hist[i] = silu_done
            hmul_hist[i] = hmul_done
            def vr(v, i=i, b=b, t=t_d, tw_out=out_done.get(i - 1)):
                v.wait_ge(st.h, t)
                if tw_out is not None:
                    v.wait_ge(sgp.h, tw_out)
                last = None
                for jn in range(2):
                    sl = slice(jn * 512, jn * 512 + 512)
                    last = v.scalar_tensor_tensor(out=self.x1t[:, sl], in0=self.psd[jn][:, :512],
                                                  scalar=0.5, in1=self.xt3[bx][:, sl],
                                                  op0=ALU.mult, op1=ALU.add)
                last.then_inc(sv.h, 1)
            self.E("vector", vr)
            resid_done[i] = sv.inc()
            def stx(g, i=i, t=resid_done[i]):
                g.wait_ge(sv.h, t)
                self.dma(g, sgp, dst_hbm[i * 128:(i + 1) * 128, :], self.x1t)
            self.E("gpsimd", stx)
            out_done[i] = sgp.inc(16)

        stage1(0)
        emit_xq_tp(0)
        stage1(1)
        emit_xq_tp(1)
        for i in range(NT):
            hook = (lambda j=i + 2: stage1(j)) if i + 2 < NT else None
            stage2(i, mid_hook=hook)

    def phase_b1(self):
        sin, sv, st, ss, sgp = self.sin, self.sv, self.st, self.ss, self.sgp
        def gw(g):
            self.dma(g, sgp, self.Wk_sb.rearrange("p (kc n) -> p kc n", n=D), self.Wk_in.rearrange("(kc p) n -> p kc n", p=128))
            self.dma(g, sgp, self.Wv_sb.rearrange("p (kc n) -> p kc n", n=D), self.Wv_in.rearrange("(kc p) n -> p kc n", p=128))
        self.E("gpsimd", gw)
        t_w = sgp.inc(32)
        prev_evac = {}
        for i in range(NT):
            b = i % 2
            def ld(sy, i=i, b=b, tw=prev_evac.get(i - 2)):
                if tw is not None:
                    sy.wait_ge(ss.h, tw)
                self.dma(sy, sin, self.xt[b], self.x1_hbm[i * 128:(i + 1) * 128, :])
            self.E("sync", ld)
            t_x = sin.inc(16)
            def vw(v, t=t_x):
                v.wait_ge(sin.h, t)
            self.E("vector", vw)
            self.emit_ln(self.xt[b], self.xln_bf, s_extra_waits=[(sin.h, t_x)],
                         sqscr=self.lnsq)
            def vq(v):
                v.engine_nop().then_inc(sv.h, 1)
            self.E("vector", vq)
            t_ln = sv.inc()
            srcs = [self.xln_bf[:, j * 128:(j + 1) * 128] for j in range(NKD)]
            tws = [(sv.h, t_ln), (ss.h, ss.c)]
            if i == 0:
                tws.append((sgp.h, t_w))
            t_xlnT = self.emit_transpose_batch(srcs, self.xlnT, twaits=tws)
            for half in range(2):
                pbank = self.psg[half]
                def tk(t, half=half, t_xlnT=t_xlnT, pbank=pbank,
                       twp=prev_evac.get(i - 1)):
                    t.wait_ge(ss.h, t_xlnT)
                    if twp is not None and half == 0:
                        t.wait_ge(ss.h, twp)
                    last = None
                    for kc in range(NKD):
                        for nl in range(4):
                            nc_ = half * 4 + nl
                            last = t.matmul(pbank[:, nl * 128:(nl + 1) * 128],
                                            self.Wk_sb[:, kc * D + nc_ * 128: kc * D + nc_ * 128 + 128],
                                            self.xlnT[:, kc * 128:(kc + 1) * 128],
                                            start=(kc == 0 and nl == 0),
                                            stop=(kc == NKD - 1 and nl == 3))
                    last.then_inc(st.h, 1)
                self.E("tensor", tk)
                t_k = st.inc()
                def sk(s, i=i, half=half, pbank=pbank, t=t_k):
                    s.wait_ge(st.h, t)
                    last = None
                    for nl in range(4):
                        nc_ = half * 4 + nl
                        last = s.activation(
                            out=self.k_sb[:, nc_ * S + i * 128: nc_ * S + i * 128 + 128],
                            in_=pbank[:, nl * 128:(nl + 1) * 128],
                            func=ACT.Relu, bias=self.bkp[:, nc_:nc_ + 1])
                    last.then_inc(ss.h, 1)
                self.E("scalar", sk)
                ss.inc()
            def tv(t, t_xlnT=t_xlnT, twp=prev_evac.get(i - 1)):
                if twp is not None:
                    t.wait_ge(ss.h, twp)
                for kc in range(NKD):
                    for jh in range(2):
                        t.matmul(self.psu[jh][:, :512],
                                 self.xlnT[:, kc * 128:(kc + 1) * 128],
                                 self.Wv_sb[:, kc * D + jh * 512: kc * D + jh * 512 + 512],
                                 start=(kc == 0), stop=False)
                last = None
                for jh in range(2):
                    last = t.matmul(self.psu[jh][:, :512], self.onesb,
                                    self.bvb[:, jh * 512:jh * 512 + 512],
                                    start=False, stop=True)
                last.then_inc(st.h, 1)
            self.E("tensor", tv)
            t_v = st.inc()
            def sv_(s, i=i, t=t_v):
                s.wait_ge(st.h, t)
                last = None
                for jh in range(2):
                    last = s.activation(
                        out=self.v_sb[:, i * D + jh * 512: i * D + jh * 512 + 512],
                        in_=self.psu[jh][:, :512], func=ACT.Relu)
                last.then_inc(ss.h, 1)
            self.E("scalar", sv_)
            prev_evac[i] = ss.inc()

    def phase_b2(self):
        sin, sv, st, ss, sgp = self.sin, self.sv, self.st, self.ss, self.sgp
        def gw(g):
            self.dma(g, sgp, self.Wq_sb.rearrange("p (kc n) -> p kc n", n=D),
                     self.Wq_in.rearrange("(kc p) n -> p kc n", p=128))
            self.dma(g, sgp, self.Wo_sb.rearrange("p (kc n) -> p kc n", n=D),
                     self.Wo_in.rearrange("(kc p) n -> p kc n", p=128))
        self.E("gpsimd", gw)
        t_w = sgp.inc(32)
        x2_out = {}
        t_qdone = {}
        t_x2v = {}
        t_yv = {}
        t_xlnTd = {}
        pend_ln = {}

        def emit_xln_tp(j):
            if j not in pend_ln or j in t_xlnTd:
                return
            srcs = [self.xln_bf[:, c * 128:(c + 1) * 128] for c in range(NKD)]
            tws = [(sv.h, pend_ln[j]), (ss.h, ss.c)]
            if j == 0:
                tws.append((sgp.h, t_w))
            t_xlnTd[j] = self.emit_transpose_batch(srcs, self.xlnT, twaits=tws)

        def s1(i):
            b = i % 2
            def ld(sy, i=i, b=b, tw=x2_out.get(i - 2)):
                if tw is not None:
                    sy.wait_ge(sgp.h, tw)
                self.dma(sy, sin, self.xt[b], self.x1_hbm[i * 128:(i + 1) * 128, :])
            self.E("sync", ld)
            t_x = sin.inc(16)
            def vw(v, t=t_x, te=t_xlnTd.get(i - 1)):
                v.wait_ge(sin.h, t)
                if te is not None:
                    v.wait_ge(ss.h, te)
            self.E("vector", vw)
            self.emit_ln(self.xt[b], self.xln_bf, s_extra_waits=[(sin.h, t_x)])
            def vq(v):
                v.engine_nop().then_inc(sv.h, 1)
            self.E("vector", vq)
            pend_ln[i] = sv.inc()
        def qproj(i):
            b = i % 2
            for half in range(2):
                pbank = self.psg[half]
                def tq(t, half=half, b=b, t_xlnT=t_xlnTd[i], pbank=pbank,
                       twp=t_qdone.get(i - 1)):
                    t.wait_ge(ss.h, t_xlnT)
                    if twp is not None and half == 0:
                        t.wait_ge(ss.h, twp)
                    last = None
                    for kc in range(NKD):
                        for nl in range(4):
                            nc_ = half * 4 + nl
                            last = t.matmul(pbank[:, nl * 128:(nl + 1) * 128],
                                            self.Wq_sb[:, kc * D + nc_ * 128: kc * D + nc_ * 128 + 128],
                                            self.xlnT[:, kc * 128:(kc + 1) * 128],
                                            start=(kc == 0 and nl == 0),
                                            stop=(kc == NKD - 1 and nl == 3))
                    last.then_inc(st.h, 1)
                self.E("tensor", tq)
                t_q = st.inc()
                def sq(s, half=half, b=b, pbank=pbank, t=t_q):
                    s.wait_ge(st.h, t)
                    last = None
                    for nl in range(4):
                        nc_ = half * 4 + nl
                        last = s.activation(out=self.q_fm[:, nc_ * 128:(nc_ + 1) * 128],
                                            in_=pbank[:, nl * 128:(nl + 1) * 128],
                                            func=ACT.Relu, scale=1.0 / 32.0,
                                            bias=self.bqp32[:, nc_:nc_ + 1])
                    last.then_inc(ss.h, 1)
                self.E("scalar", sq)
                t_qdone[i] = ss.inc()

        qk_evacs = {}

        def emit_qk(i, g0_start, extra_evac_waits=(), extra_t_waits=()):
            b = i % 2
            W = (i + 1) * 128
            ngrp = (W + 511) // 512
            qk_evac = qk_evacs.setdefault(i, {})
            for g0 in range(g0_start, ngrp, 2):
                gpair = tuple(g for g in (g0, g0 + 1) if g < ngrp)
                if gpair[0] in qk_evac:
                    continue
                def tqk(t, gpair=gpair, W=W, tq_=t_qdone[i],
                        twp=qk_evac.get(g0 - 2),
                        twx=(t_x2v.get(i - 1) if g0 == 0 else None),
                        etw=(tuple(extra_t_waits) if g0 == g0_start else ())):
                    t.wait_ge(ss.h, tq_)
                    for h, c in etw:
                        t.wait_ge(h, c)
                    if twx is not None:
                        t.wait_ge(sv.h, twx)
                    if twp is not None:
                        t.wait_ge(sv.h, twp)
                    last = None
                    for kc in range(NKD):
                        for g_ in gpair:
                            wg_ = min(512, W - g_ * 512)
                            last = t.matmul(self.psu[g_ % 2][:, :wg_],
                                            self.q_fm[:, kc * 128:(kc + 1) * 128],
                                            self.k_sb[:, kc * S + g_ * 512: kc * S + g_ * 512 + wg_],
                                            start=(kc == 0), stop=(kc == NKD - 1))
                    last.then_inc(st.h, 1)
                self.E("tensor", tqk)
                t_qk = st.inc()
                def vqk(v, gpair=gpair, W=W, t=t_qk, ew=tuple(extra_evac_waits)):
                    v.wait_ge(st.h, t)
                    for h, c in ew:
                        v.wait_ge(h, c)
                    last = None
                    for g_ in gpair:
                        wg_ = min(512, W - g_ * 512)
                        last = v.tensor_copy(out=self.scores[:, g_ * 512:g_ * 512 + wg_],
                                             in_=self.psu[g_ % 2][:, :wg_])
                    last.then_inc(sv.h, 1)
                self.E("vector", vqk)
                t_e = sv.inc()
                for g_ in gpair:
                    qk_evac[g_] = t_e

        def s2(i):
            b = i % 2
            W = (i + 1) * 128
            ngrp = (W + 511) // 512
            emit_qk(i, 2)
            qk_evac = qk_evacs[i]
            def vmask(v, i=i, W=W):
                v.tensor_tensor(out=self.scores[:, i * 128:W], in0=self.scores[:, i * 128:W],
                                in1=self.causal, op=ALU.subtract)
                v.tensor_reduce(out=self.negmax, in_=self.scores[:, 0:W], axis=AX.X,
                                op=ALU.max, negate=True)
                v.engine_nop().then_inc(sv.h, 1)
            self.E("vector", vmask)
            t_mask = sv.inc()
            def sexp(s, W=W, t=t_mask):
                s.wait_ge(sv.h, t)
                s.activation(out=self.attn_bf[:, 0:W], in_=self.scores[:, 0:W],
                             func=ACT.Exp, bias=self.negmax,
                             accum_out=self.rowsum).then_inc(ss.h, 1)
            self.E("scalar", sexp)
            t_exp = ss.inc()
            if i + 1 < NT:
                emit_xln_tp(i + 1)
                qproj(i + 1)
                own = qk_evacs.get(i, {})
                t_own = max(own.values()) if own else 0
                emit_qk(i + 1, 0, extra_evac_waits=[(ss.h, t_exp)],
                        extra_t_waits=([(sv.h, t_own)] if t_own else []))
            def vrcp(v, t=t_exp):
                v.wait_ge(ss.h, t)
                v.reciprocal(out=self.rcpsum, in_=self.rowsum).then_inc(sv.h, 1)
            self.E("vector", vrcp)
            t_rcp = sv.inc()
            nkt = i + 1
            first = True
            pv_prev = None
            for b0 in range(0, nkt, 8):
                nb = min(8, nkt - b0)
                srcs = [self.attn_bf[:, (b0 + j) * 128:(b0 + j + 1) * 128] for j in range(nb)]
                sws = [] if pv_prev is None else [(st.h, pv_prev)]
                t_aT = self.emit_transpose_batch(srcs, self.attnT[:, :nb * 128],
                                                 twaits=[(ss.h, max(t_exp, ss.c))],
                                                 swaits=sws)
                def tpv(t, b0=b0, nb=nb, t_aT=t_aT, first=first,
                        last_batch=(b0 + nb == nkt),
                        twy=(t_yv.get(i - 1) if first else None)):
                    t.wait_ge(ss.h, t_aT)
                    if twy is not None:
                        t.wait_ge(sv.h, twy)
                    last = None
                    for j in range(nb):
                        for jh in range(2):
                            last = t.matmul(self.psd[jh][:, :512],
                                            self.attnT[:, j * 128:(j + 1) * 128],
                                            self.v_sb[:, (b0 + j) * D + jh * 512:
                                                      (b0 + j) * D + jh * 512 + 512],
                                            start=(first and j == 0),
                                            stop=(last_batch and j == nb - 1))
                    last.then_inc(st.h, 1)
                self.E("tensor", tpv)
                pv_prev = st.inc()
                first = False
            def vy(v, t=pv_prev, trc=t_rcp):
                v.wait_ge(st.h, t)
                v.wait_ge(sv.h, trc)
                self.vgap(v)
                self.vgap(v)
                for jh in range(2):
                    v.tensor_scalar_mul(out=self.y_bf[:, jh * 512:(jh + 1) * 512],
                                        in0=self.psd[jh][:, :512], scalar1=self.rcpsum)
                v.engine_nop().then_inc(sv.h, 1)
            self.E("vector", vy)
            t_y = sv.inc()
            t_yv[i] = t_y
            srcs = [self.y_bf[:, j * 128:(j + 1) * 128] for j in range(NKD)]
            t_yT = self.emit_transpose_batch(srcs, self.yT,
                                             twaits=[(sv.h, t_y), (ss.h, ss.c)])
            hoisted = qk_evacs.get(i + 1, {})
            t_hoist = max(hoisted.values()) if hoisted else None
            def to(t, t_yT=t_yT, th=t_hoist):
                t.wait_ge(ss.h, t_yT)
                if th is not None:
                    t.wait_ge(sv.h, th)
                for kc in range(NKD):
                    for jh in range(2):
                        t.matmul(self.psu[jh][:, :512],
                                 self.yT[:, kc * 128:(kc + 1) * 128],
                                 self.Wo_sb[:, kc * D + jh * 512: kc * D + jh * 512 + 512],
                                 start=(kc == 0), stop=False)
                last = None
                for jh in range(2):
                    last = t.matmul(self.psu[jh][:, :512], self.onesb,
                                    self.bob[:, jh * 512:jh * 512 + 512],
                                    start=False, stop=True)
                last.then_inc(st.h, 1)
            self.E("tensor", to)
            t_o = st.inc()
            def vo(v, b=b, t=t_o):
                v.wait_ge(st.h, t)
                last = None
                for jh in range(2):
                    sl = slice(jh * 512, (jh + 1) * 512)
                    last = v.scalar_tensor_tensor(out=self.xt[b][:, sl],
                                                  in0=self.psu[jh][:, :512],
                                                  scalar=0.0, in1=self.xt[b][:, sl],
                                                  op0=ALU.max, op1=ALU.add)
                last.then_inc(sv.h, 1)
            self.E("vector", vo)
            t_x2 = sv.inc()
            t_x2v[i] = t_x2
            def stx(g, i=i, b=b, t=t_x2):
                g.wait_ge(sv.h, t)
                self.dma(g, sgp, self.x2_hbm[i * 128:(i + 1) * 128, :], self.xt[b])
            self.E("gpsimd", stx)
            x2_out[i] = sgp.inc(16)

        s1(0)
        emit_xln_tp(0)
        qproj(0)
        emit_qk(0, 0)
        for i in range(NT):
            if i + 1 < NT:
                s1(i + 1)
            s2(i)


def _build_inputs(x_shard, ins):
    causal = (np.arange(128)[None, :] > np.arange(128)[:, None]).astype(np.float32) * 10000.0
    m = {
        "wg_in": ins["L0_ffn_w_gate"],
        "wu_in": ins["L0_ffn_w_up"],
        "wd_in": ins["L0_ffn_w_down"],
        "Wq_in": ins["L0_attn_Wq"],
        "Wk_in": ins["L0_attn_Wk"],
        "Wv_in": ins["L0_attn_Wv"],
        "Wo_in": ins["L0_attn_Wo"],
        "bq_in": ins["L0_attn_bq"],
        "bk_in": ins["L0_attn_bk"],
        "bv_in": ins["L0_attn_bv"],
        "bo_in": ins["L0_attn_bo"],
        "ident_in": np.eye(128, dtype=np.float32),
        "causal_in": causal,
        "ones_in": np.ones((1, 128), dtype=np.float32),
    }
    m["x_in"] = x_shard
    return m


def kernel(L0_attn_x, L0_attn_Wq, L0_attn_bq, L0_attn_Wk, L0_attn_bk,
           L0_attn_Wv, L0_attn_bv, L0_attn_Wo, L0_attn_bo,
           L0_ffn_w_gate, L0_ffn_w_up, L0_ffn_w_down):
    from concourse.bass_utils import run_bass_kernel_spmd
    ins = {k: np.ascontiguousarray(np.asarray(v, dtype=np.float32)) for k, v in {
        "L0_ffn_w_gate": L0_ffn_w_gate, "L0_ffn_w_up": L0_ffn_w_up,
        "L0_ffn_w_down": L0_ffn_w_down, "L0_attn_Wq": L0_attn_Wq,
        "L0_attn_Wk": L0_attn_Wk, "L0_attn_Wv": L0_attn_Wv,
        "L0_attn_Wo": L0_attn_Wo, "L0_attn_bq": L0_attn_bq,
        "L0_attn_bk": L0_attn_bk, "L0_attn_bv": L0_attn_bv,
        "L0_attn_bo": L0_attn_bo}.items()}
    x = np.ascontiguousarray(np.asarray(L0_attn_x, dtype=np.float32))
    kb = KB(dev_outputs=False)
    nc = kb.build(phases=("P", "A", "B1", "B2", "C"))
    mybir.codegen_inst_isa_subclasses(nc)
    in_maps = [_build_inputs(np.ascontiguousarray(x[c]), ins) for c in range(8)]
    res = run_bass_kernel_spmd(nc, in_maps, core_ids=list(range(8)))
    return np.stack([np.asarray(r["out"], dtype=np.float32) for r in res.results], axis=0)


# version 16
# speedup vs baseline: 1.0009x; 1.0009x over previous
import concourse.bass_utils as _bu
if not getattr(_bu, "_ldw_patched", False):
    _orig_gwa = _bu.get_walrus_args
    def _gwa(*a, **k):
        return [x.replace("--enable-ldw-opt=false", "--enable-ldw-opt=true")
                for x in _orig_gwa(*a, **k)]
    _bu.get_walrus_args = _gwa
    _bu._ldw_patched = True
import numpy as np
import concourse.dve_ops as dve_ops
from concourse.dve_op import DveOp
from concourse.dve_spec import Spec, Src0, Src1, C0, C1, C2, C3, Zero, Bin, select
from concourse.dve_spec import lower as spec_lower
from concourse.dve_uop import AluOp
from concourse.dve_table_gen import dve_ver_for

_VER = dve_ver_for("TRN2")


def _f(x):
    return np.float64 if getattr(x, "dtype", None) == np.float64 else np.float32


def _mk(name, body, ref, perf_en=False):
    spec = Spec(body=body)
    sha = spec_lower(spec, ver=_VER, perf_en=perf_en, name=name).sha(_VER)
    return DveOp(name, spec, subdim=False, uops_sha={_VER: sha}, reference=ref,
                 perf_en={_VER: perf_en} if perf_en else {})


_t = Bin(AluOp.MULTIPLY, Src0, Src1)
QABS = _mk("QABS_ANT", Bin(AluOp.MAX, _t, Bin(AluOp.SUBTRACT, Zero, _t)),
           lambda in0, in1, s0, s1, imm2: np.abs(in0.astype(_f(in0)) * in1))

_lo = (Bin(AluOp.IS_GT, Src0, C0) + Bin(AluOp.IS_GT, Src0, C1)) + \
      (Bin(AluOp.IS_GT, Src0, C2) + Bin(AluOp.IS_GT, Src0, C3))


def _qlo_ref(in0, in1, s0, s1, imm2):
    f = _f(in0)
    a = in0.astype(f)
    c3 = np.asarray(in1, f).reshape(-1, 1)
    while c3.ndim < a.ndim:
        c3 = c3[..., None]
    return ((a > s0).astype(f) + (a > s1) + (a > imm2) + (a > c3)).astype(f)


QLO = _mk("QLO_ANT", _lo, _qlo_ref)

_c6 = Bin(AluOp.IS_GT, Src0, C2)
_hi = ((Bin(AluOp.IS_GT, Src0, C0) + Bin(AluOp.IS_GT, Src0, C1)) + _c6) + _c6


def _qhi_ref(in0, in1, s0, s1, imm2):
    f = _f(in0)
    a = in0.astype(f)
    return ((a > s0).astype(f) + (a > s1) + 2.0 * (a > imm2)).astype(f)


QHI = _mk("QHI_ANT", _hi, _qhi_ref)

_cmb = (Src0 * C0 + C1) + Src1


def _qcmb_ref(in0, in1, s0, s1, imm2):
    f = _f(in0)
    return (in0.astype(f) * s0 + s1) + in1


QCMB = _mk("QCMB_ANT", _cmb, _qcmb_ref, perf_en=True)

_sg = select(Bin(AluOp.IS_LT, Src1, Zero), Bin(AluOp.SUBTRACT, Zero, Src0), Src0)


def _qsgn_ref(in0, in1, s0, s1, imm2):
    f = _f(in0)
    a = in0.astype(f)
    return np.where(in1 < 0, -a, a).astype(f)


QSGN = _mk("QSGN_ANT", _sg, _qsgn_ref)

_VICTIMS = ("DVEHASH32L_SEED", "DVEHASH32L_BODY", "DVEHASH32L_DRAIN",
            "BITWISE_XOR_TT_BV", "SHIFT_RIGHT_TT_BV")
_NEW = (QABS, QLO, QHI, QCMB, QSGN)


def register():
    if QABS.name in dve_ops._SUB_OPCODE_FOR_NAME:
        return
    for victim, op in zip(_VICTIMS, _NEW):
        idx = next(i for i, o in enumerate(dve_ops.OPS) if o.name == victim)
        row = dve_ops._SUB_OPCODE_FOR_NAME.pop(victim)
        dve_ops.OPS[idx] = op
        dve_ops._SUB_OPCODE_FOR_NAME[op.name] = row
        dve_ops.CUSTOM_DVE_SPECS[op.name] = op.spec
        dve_ops.CUSTOM_DVE_REFS[op.name] = op.reference


import numpy as np
from contextlib import ExitStack
import concourse.bass as bass
import concourse.mybir as mybir

F32 = mybir.dt.float32
BF16 = mybir.dt.bfloat16
ALU = mybir.AluOpType
ACT = mybir.ActivationFunctionType
AX = mybir.AxisListType

S = 4096
D = 1024
FI = 2816
NT = S // 128
NKD = D // 128
NKI = FI // 128
MIDS = (0.25, 0.75, 1.25, 1.75, 2.5, 3.5, 5.0)
DELTAS = (0.5, 0.5, 0.5, 0.5, 1.0, 1.0, 2.0)
QCHUNK = 704

ENGINES = ("sync", "gpsimd", "vector", "scalar", "tensor")

_register_qops = register


class Sem:
    __slots__ = ("h", "c")
    def __init__(self, h):
        self.h = h
        self.c = 0
    def inc(self, n=1):
        self.c += n
        return self.c


class KB:
    def __init__(self, dev_outputs=False):
        self.dev = dev_outputs
        self.nc = bass.Bass("TRN2", target_bir_lowering=False, debug=False,
                            detect_race_conditions=False)
        self.es = ExitStack()
        self.ops = {e: [] for e in ENGINES}
        self._uid = 0
        self._dmac = {}
        self._sect = {}

    def uid(self):
        self._uid += 1
        return str(self._uid)

    def E(self, eng, fn):
        self.ops[eng].append(fn)

    def dma(self, eng, sem, out, in_, **kw):
        wv = self._dmac.get(id(sem), 0)
        self._dmac[id(sem)] = wv + 16
        if wv:
            eng.wait_ge(sem.h, wv)
        eng.dma_start(out=out, in_=in_, **kw).then_inc(sem.h, 16)

    def vgap(self, v):
        v.tensor_scalar(out=self.vgapb, in0=self.vgapb, scalar1=1.0,
                        scalar2=None, op0=ALU.mult)

    def tap(self, name, ap, waits=()):
        if not self.dev:
            return None
        t = self.nc.dram_tensor("tap_" + name, list(ap.shape), ap.dtype,
                                kind="ExternalOutput").ap()
        def g(gp, t=t, ap=ap, waits=tuple(waits)):
            for h, c in waits:
                gp.wait_ge(h, c)
            self.dma(gp, self.sgp, t, ap)
        self.E("gpsimd", g)
        return self.sgp.inc(16)

    def flush(self):
        for eng in ENGINES:
            lst = self.ops[eng]
            if not lst:
                continue
            head = None
            tail = None
            if eng == "vector":
                prev = self._sect.get(eng)
                if prev is not None:
                    head = lambda v, c=prev: v.wait_ge(self.sv.h, c)
                tail = lambda v: v.engine_nop().then_inc(self.sv.h, 1)
                self._sect[eng] = self.sv.inc()
            elif eng == "scalar":
                prev = self._sect.get(eng)
                if prev is not None:
                    head = lambda s, c=prev: s.wait_ge(self.ss.h, c)
                tail = lambda s: s.copy(out=self.sgapb,
                                        in_=self.epsv).then_inc(self.ss.h, 1)
                self._sect[eng] = self.ss.inc()
            elif eng == "tensor":
                prev = self._sect.get(eng)
                if prev is not None:
                    head = lambda t, c=prev: t.wait_ge(self.st.h, c)
                sev = self.ss.c
                def tail(t, sev=sev):
                    t.wait_ge(self.ss.h, sev)
                    t.transpose(self.psT[:, 0:128], self.ident,
                                self.ident).then_inc(self.st.h, 1)
                self._sect[eng] = self.st.inc()
            def runner(e, lst=lst, head=head, tail=tail):
                if head is not None:
                    head(e)
                for f in lst:
                    f(e)
                if tail is not None:
                    tail(e)
            getattr(self.block, eng)(runner)
        self.ops = {e: [] for e in ENGINES}

    def barrier(self):
        self.flush()
        snaps = [(s.h, s.c) for s in (self.sin, self.sgp, self.sv, self.ss, self.st) if s.c > 0]
        for eng in ENGINES:
            def w(e, snaps=tuple(snaps)):
                for h, c in snaps:
                    e.wait_ge(h, c)
            self.E(eng, w)

    def build(self, phases=("P", "A", "B1", "B2", "C")):
        _register_qops()
        nc = self.nc
        T = lambda n, shp, dt, kind: nc.dram_tensor(n, shp, dt, kind=kind).ap()
        self.x_in = T("x_in", [S, D], F32, "ExternalInput")
        self.wg_in = T("wg_in", [FI, D], F32, "ExternalInput")
        self.wu_in = T("wu_in", [FI, D], F32, "ExternalInput")
        self.wd_in = T("wd_in", [D, FI], F32, "ExternalInput")
        self.Wq_in = T("Wq_in", [D, D], F32, "ExternalInput")
        self.Wk_in = T("Wk_in", [D, D], F32, "ExternalInput")
        self.Wv_in = T("Wv_in", [D, D], F32, "ExternalInput")
        self.Wo_in = T("Wo_in", [D, D], F32, "ExternalInput")
        self.bq_in = T("bq_in", [D], F32, "ExternalInput")
        self.bk_in = T("bk_in", [D], F32, "ExternalInput")
        self.bv_in = T("bv_in", [D], F32, "ExternalInput")
        self.bo_in = T("bo_in", [D], F32, "ExternalInput")
        self.ident_in = T("ident_in", [128, 128], F32, "ExternalInput")
        self.causal_in = T("causal_in", [128, 128], F32, "ExternalInput")
        self.ones_in = T("ones_in", [1, 128], F32, "ExternalInput")
        sk = "ExternalOutput" if self.dev else "Internal"
        self.x1_hbm = T("x1_hbm", [S, D], F32, sk)
        self.x2_hbm = T("x2_hbm", [S, D], F32, sk)
        self.wgT_hbm = T("wgT_hbm", [128, NKD * FI], BF16, "Internal")
        self.wuT_hbm = T("wuT_hbm", [128, NKD * FI], BF16, "Internal")
        self.wdT_hbm = T("wdT_hbm", [128, NKI * D], BF16, "Internal")
        self.out = T("out", [S, D], F32, "ExternalOutput")

        es = self.es
        sb = lambda n, shp, dt: es.enter_context(nc.sbuf_tensor(n, shp, dt))[:]
        psf = lambda n, shp, dt: es.enter_context(nc.psum_tensor(n, shp, dt))[:]
        sem = lambda n: Sem(es.enter_context(nc.semaphore(n)))

        self.sin = sem("sin")
        self.sgp = sem("sgp")
        self.sv = sem("sv")
        self.ss = sem("ss")
        self.st = sem("st")

        self.ident = sb("ident", [128, 128], BF16)
        self.causal = sb("causal", [128, 128], F32)
        self.onesb = sb("onesb", [1, 128], BF16)
        self.bvb = sb("bvb", [1, D], BF16)
        self.bob = sb("bob", [1, D], BF16)
        self.bkp = sb("bkp", [128, NKD], F32)
        self.bqp32 = sb("bqp32", [128, NKD], F32)
        self.xt = [sb("xt0", [128, D], F32), sb("xt1", [128, D], F32)]
        self.lnb = {k: sb("ln_" + k, [128, 1], F32)
                    for k in ("sm", "m", "ssum", "msq", "var", "sqv", "rstd")}
        self.epsv = sb("epsv", [128, 1], F32)
        self.c175 = sb("c175", [128, 1], F32)
        self.vgapb = sb("vgapb", [128, 32], F32)
        self.sgapb = sb("sgapb", [128, 1], F32)
        self.psT = psf("psT", [128, 8 * 128], BF16)
        self.psg = [psf("psg0", [128, 512], F32), psf("psg1", [128, 512], F32)]
        self.psu = [psf("psu0", [128, 512], F32), psf("psu1", [128, 512], F32)]
        self.psd = [psf("psd0", [128, 512], F32), psf("psd1", [128, 512], F32)]

        self.block = es.enter_context(nc.Block())
        sin, sgp, sv = self.sin, self.sgp, self.sv

        def g0(g):
            self.dma(g, sgp, self.ident, self.ident_in)
            self.dma(g, sgp, self.onesb, self.ones_in)
            self.dma(g, sgp, self.bvb, self.bv_in.unsqueeze(0))
            self.dma(g, sgp, self.bob, self.bo_in.unsqueeze(0))
        self.E("gpsimd", g0)
        sgp.inc(64)
        self._ident_ready = 16
        def s0(sy):
            self.dma(sy, sin, self.causal, self.causal_in)
            self.dma(sy, sin, self.bkp, self.bk_in.rearrange("(c p) -> p c", p=128),
                         allow_slow_non_contiguous=True)
            self.dma(sy, sin, self.bqp32, self.bq_in.rearrange("(c p) -> p c", p=128),
                         allow_slow_non_contiguous=True)
        self.E("sync", s0)
        sin.inc(48)
        t_bq = sin.c
        def v0(v, t=t_bq):
            v.wait_ge(sin.h, t)
            z1 = self.bqp32[:, 0:1]
            v.tensor_scalar(out=self.epsv, in0=z1, scalar1=0.0, scalar2=1e-5,
                            op0=ALU.mult, op1=ALU.add)
            v.tensor_scalar(out=self.c175, in0=z1, scalar1=0.0, scalar2=1.75,
                            op0=ALU.mult, op1=ALU.add)
            v.tensor_scalar(out=self.vgapb, in0=z1.broadcast_to([128, 32]),
                            scalar1=0.0, scalar2=0.0, op0=ALU.mult, op1=ALU.add)
            v.tensor_scalar(out=self.bqp32, in0=self.bqp32, scalar1=1.0 / 32.0,
                            scalar2=None, op0=ALU.mult).then_inc(sv.h, 1)
        self.E("vector", v0)
        sv.inc()

        if "P" in phases or "A" in phases:
            with ExitStack() as pes:
                self.ffn_wbufs(pes)
                if "P" in phases:
                    with ExitStack() as pro_es:
                        u2 = self.uid()
                        self.wstage = pro_es.enter_context(
                            nc.sbuf_tensor("wstage" + u2, [128, FI], F32))[:]
                        self.wqnat = pro_es.enter_context(
                            nc.sbuf_tensor("wqnat" + u2, [128, FI], BF16))[:]
                        self.phase_prologue()
                        self.barrier()
                if "A" in phases:
                    self.ffn_tilebufs(pes)
                    self.phase_ffn(self.x_in, self.x1_hbm)
                    self.barrier()
        if "B1" in phases or "B2" in phases:
            with ExitStack() as pes:
                sbp = lambda n, shp, dt: pes.enter_context(nc.sbuf_tensor(n, shp, dt))[:]
                self.k_sb = sbp("k_sb", [128, NKD * S], BF16)
                self.v_sb = sbp("v_sb", [128, NT * D], BF16)
                self.xln_bf = sbp("xln_bf", [128, D], BF16)
                self.xlnT = sbp("xlnT", [128, NKD * 128], BF16)
                if "B1" in phases:
                    with ExitStack() as b1es:
                        sb1 = lambda n, shp, dt: b1es.enter_context(nc.sbuf_tensor(n, shp, dt))[:]
                        self.Wk_sb = sb1("Wk_sb", [128, NKD * D], BF16)
                        self.Wv_sb = sb1("Wv_sb", [128, NKD * D], BF16)
                        self.lnsq = sb1("lnsq_b", [128, D], F32)
                        self.phase_b1()
                        self.barrier()
                if "B2" in phases:
                    with ExitStack() as b2es:
                        sb2 = lambda n, shp, dt: b2es.enter_context(nc.sbuf_tensor(n, shp, dt))[:]
                        self.Wq_sb = sb2("Wq_sb", [128, NKD * D], BF16)
                        self.Wo_sb = sb2("Wo_sb", [128, NKD * D], BF16)
                        self.scores = sb2("scores", [128, S], F32)
                        self.attn_bf = sb2("attn_bf", [128, S], BF16)
                        self.q_fm = sb2("q_fm", [128, NKD * 128], BF16)

                        self.rowsum = sb2("rowsum", [128, 1], F32)
                        self.negmax = sb2("negmax", [128, 1], F32)
                        self.rcpsum = sb2("rcpsum", [128, 1], F32)
                        self.attnT = sb2("attnT", [128, 8 * 128], BF16)
                        self.y_bf = self.attn_bf[:, 0:D]
                        self.yT = self.attn_bf[:, D:2 * D]
                        self.phase_b2()
                        self.barrier()
        if "C" in phases:
            with ExitStack() as pes:
                self.ffn_wbufs(pes)
                self.ffn_tilebufs(pes)
                def ldw(sy):
                    self.dma(sy, self.sin, self.wgT, self.wgT_hbm)
                    self.dma(sy, self.sin, self.wuT, self.wuT_hbm)
                    self.dma(sy, self.sin, self.wdT, self.wdT_hbm)
                self.E("sync", ldw)
                self.sin.inc(48)
                wq_ready = self.sin.c
                self.phase_ffn(self.x2_hbm, self.out, wq_wait=wq_ready)
                self.flush()
        def fin(g):
            g.wait_ge(self.sgp.h, self.sgp.c)
        self.E("gpsimd", fin)
        self.flush()
        self.es.close()
        return nc

    def ffn_wbufs(self, pes):
        nc = self.nc
        u = self.uid()
        sbp = lambda n, shp, dt: pes.enter_context(nc.sbuf_tensor(n + u, shp, dt))[:]
        self.wgT = sbp("wgT", [128, NKD * FI], BF16)
        self.wuT = sbp("wuT", [128, NKD * FI], BF16)
        self.wdT = sbp("wdT", [128, NKI * D], BF16)
        self.q = {
            "absz": sbp("q_absz", [128, QCHUNK], F32),
            "an": sbp("q_an", [128, QCHUNK], F32),
            "ca": sbp("q_ca", [128, QCHUNK], BF16),
            "cb": sbp("q_cb", [128, QCHUNK], BF16),
            "gm": sbp("q_gm", [128, QCHUNK // 16], F32),
            "sc": sbp("q_sc", [128, QCHUNK // 16], F32),
            "rcp": sbp("q_rcp", [128, QCHUNK // 16], F32),
            "sg16": sbp("q_sg16", [128, QCHUNK], BF16),
        }

    def ffn_tilebufs(self, pes):
        nc = self.nc
        u = self.uid()
        sbp = lambda n, shp, dt: pes.enter_context(nc.sbuf_tensor(n + u, shp, dt))[:]
        self.xln = sbp("xln", [128, D], F32)
        self.xq = [sbp("xq0", [128, D], BF16), sbp("xq1", [128, D], BF16)]
        self.xt3 = [self.xt[0], self.xt[1], sbp("xt2", [128, D], F32),
                    sbp("xt3", [128, D], F32)]
        self.xqT = [sbp("xqT0", [128, NKD * 128], BF16),
                    sbp("xqT1", [128, NKD * 128], BF16),
                    sbp("xqT2", [128, NKD * 128], BF16)]
        self.sg_sb = sbp("sg_sb", [128, FI], F32)
        self.hq = sbp("hq", [128, FI], BF16)
        self.hqT = sbp("hqT", [128, NKI * 128], BF16)
        self.x1t = sbp("x1t", [128, D], F32)

    def emit_quant(self, z_ap, out_ap, width, pre_waits=()):
        q = self.q
        nchunks = (width + QCHUNK - 1) // QCHUNK
        for c in range(nchunks):
            w = min(QCHUNK, width - c * QCHUNK)
            g = w // 16
            z = z_ap[:, c * QCHUNK:c * QCHUNK + w]
            o = out_ap[:, c * QCHUNK:c * QCHUNK + w]
            pw = tuple(pre_waits) if c == 0 else ()
            def f(v, z=z, o=o, w=w, g=g, pw=pw):
                for h, cnt in pw:
                    v.wait_ge(h, cnt)
                an = q["an"][:, :w]; qa = q["absz"][:, :w]
                ca = q["ca"][:, :w]; cb = q["cb"][:, :w]; sg16 = q["sg16"][:, :w]
                gm = q["gm"][:, :g]; sc = q["sc"][:, :g]; rcp = q["rcp"][:, :g]
                v.tensor_reduce(out=gm, in_=z.rearrange("p (g k) -> p g k", k=16),
                                axis=AX.X, op=ALU.max, apply_absolute_value=True)
                self.vgap(v)
                v.tensor_scalar(out=sc, in0=gm, scalar1=1.0 / 6.0, scalar2=1e-12,
                                op0=ALU.mult, op1=ALU.max)
                self.vgap(v)
                v.reciprocal_approx_fast(out=rcp, in_=sc)
                self.vgap(v)
                rb = rcp.unsqueeze(2).broadcast_to([128, g, 16])
                v._custom_dve(QABS, out=an.rearrange("p (g k) -> p g k", k=16),
                              in0=z.rearrange("p (g k) -> p g k", k=16), in1=rb)
                v._custom_dve(QLO, out=ca, in0=an, in1=self.c175,
                              s0=MIDS[0], s1=MIDS[1], imm2=MIDS[2])
                v._custom_dve(QHI, out=cb, in0=an,
                              s0=MIDS[4], s1=MIDS[5], imm2=MIDS[6])
                v._custom_dve(QCMB, out=sg16, in0=ca, in1=cb, s0=0.5, s1=0.0)
                sbc = sc.unsqueeze(2).broadcast_to([128, g, 16])
                v.tensor_tensor(out=qa.rearrange("p (g k) -> p g k", k=16),
                                in0=sg16.rearrange("p (g k) -> p g k", k=16),
                                in1=sbc, op=ALU.mult)
                v._custom_dve(QSGN, out=o, in0=qa, in1=z)
            self.E("vector", f)

    def emit_ln_stats(self, xt, sqscr, pre_waits=()):
        lnb = self.lnb
        sv = self.sv
        def v1(v, pw=tuple(pre_waits)):
            for h, c in pw:
                v.wait_ge(h, c)
            v.reduce_sum(out=lnb["sm"], in_=xt, axis=AX.X)
            v.scalar_tensor_tensor(out=sqscr, in0=xt, scalar=1.0, in1=xt,
                                   op0=ALU.mult, op1=ALU.mult,
                                   accum_out=lnb["ssum"])
            v.tensor_scalar(out=lnb["m"], in0=lnb["sm"], scalar1=1.0 / D,
                            scalar2=None, op0=ALU.mult)
            v.tensor_scalar(out=lnb["var"], in0=lnb["ssum"], scalar1=1.0 / D,
                            scalar2=None, op0=ALU.mult)
            v.tensor_tensor(out=lnb["msq"], in0=lnb["m"], in1=lnb["m"], op=ALU.mult)
            self.vgap(v)
            self.vgap(v)
            v.tensor_tensor(out=lnb["var"], in0=lnb["var"], in1=lnb["msq"],
                            op=ALU.subtract).then_inc(sv.h, 1)
        self.E("vector", v1)
        t_var = sv.inc()
        def s2(s, t=t_var):
            s.wait_ge(sv.h, t)
            s.activation(out=lnb["sqv"], in_=lnb["var"], func=ACT.Sqrt,
                         bias=self.epsv).then_inc(self.ss.h, 1)
        self.E("scalar", s2)
        return self.ss.inc()

    def emit_ln_norm(self, xt, xln_out, t_sqv, pre_waits=()):
        lnb = self.lnb
        def v3(v, t=t_sqv, pw=tuple(pre_waits)):
            v.wait_ge(self.ss.h, t)
            for h, c in pw:
                v.wait_ge(h, c)
            v.reciprocal(out=lnb["rstd"], in_=lnb["sqv"])
            self.vgap(v)
            self.vgap(v)
            v.tensor_scalar(out=xln_out, in0=xt, scalar1=lnb["m"],
                            scalar2=lnb["rstd"], op0=ALU.subtract, op1=ALU.mult)
        self.E("vector", v3)

    def emit_ln(self, xt, xln_out, s_extra_waits=(), sqscr=None):
        lnb = self.lnb
        sv, ss = self.sv, self.ss
        if sqscr is None:
            sqscr = xln_out
        def v1(v, sqscr=sqscr):
            v.reduce_sum(out=lnb["sm"], in_=xt, axis=AX.X)
            v.scalar_tensor_tensor(out=sqscr, in0=xt, scalar=1.0, in1=xt,
                                   op0=ALU.mult, op1=ALU.mult,
                                   accum_out=lnb["ssum"])
            v.tensor_scalar(out=lnb["m"], in0=lnb["sm"], scalar1=1.0 / D,
                            scalar2=None, op0=ALU.mult)
            v.tensor_scalar(out=lnb["var"], in0=lnb["ssum"], scalar1=1.0 / D,
                            scalar2=None, op0=ALU.mult)
            v.tensor_tensor(out=lnb["msq"], in0=lnb["m"], in1=lnb["m"], op=ALU.mult)
            self.vgap(v)
            self.vgap(v)
            v.tensor_tensor(out=lnb["var"], in0=lnb["var"], in1=lnb["msq"],
                            op=ALU.subtract).then_inc(sv.h, 1)
        self.E("vector", v1)
        t_var = sv.inc()
        def s2(s, t=t_var, sw=tuple(s_extra_waits)):
            for h, c in sw:
                s.wait_ge(h, c)
            s.wait_ge(sv.h, t)
            s.activation(out=lnb["sqv"], in_=lnb["var"], func=ACT.Sqrt,
                         bias=self.epsv).then_inc(ss.h, 1)
        self.E("scalar", s2)
        t_sqv = ss.inc()
        def v3(v, t=t_sqv):
            v.wait_ge(ss.h, t)
            v.reciprocal(out=lnb["rstd"], in_=lnb["sqv"])
            self.vgap(v)
            self.vgap(v)
            v.tensor_scalar(out=xln_out, in0=xt, scalar1=lnb["m"],
                            scalar2=lnb["rstd"], op0=ALU.subtract, op1=ALU.mult)
        self.E("vector", v3)

    def emit_transpose_batch(self, srcs, dst_ap, twaits=(), swaits=()):
        st, ss = self.st, self.ss
        nb = len(srcs)
        if self._ident_ready is not None:
            twaits = list(twaits) + [(self.sgp.h, self._ident_ready)]
            self._ident_ready = None
        def t1(t, srcs=tuple(srcs), twaits=tuple(twaits)):
            for h, c in twaits:
                t.wait_ge(h, c)
            last = None
            for j, s in enumerate(srcs):
                last = t.transpose(self.psT[:, j * 128:(j + 1) * 128], s, self.ident)
            last.then_inc(st.h, 1)
        self.E("tensor", t1)
        t_tp = st.inc()
        def s1(s, dst_ap=dst_ap, nb=nb, t=t_tp, swaits=tuple(swaits)):
            s.wait_ge(st.h, t)
            for h, c in swaits:
                s.wait_ge(h, c)
            src = self.psT[:, :nb * 128]
            if dst_ap.ndim == 3:
                src = src.rearrange("p (j f) -> p j f", f=128)
            s.copy(out=dst_ap, in_=src).then_inc(ss.h, 1)
        self.E("scalar", s1)
        return ss.inc()

    def phase_prologue(self):
        sin, sv, st, ss, sgp = self.sin, self.sv, self.st, self.ss, self.sgp
        jobs = [(self.wg_in, NKI, NKD, FI, self.wgT, self.wgT_hbm),
                (self.wu_in, NKI, NKD, FI, self.wuT, self.wuT_hbm),
                (self.wd_in, NKD, NKI, D, self.wdT, self.wdT_hbm)]
        last_tp = 0
        quant_done = 0
        for (win, nchunks, nkc, tcols, wT, wT_hbm) in jobs:
            rowlen = win.shape[1]
            for nch in range(nchunks):
                def ld(sy, win=win, nch=nch, rowlen=rowlen, tw=quant_done):
                    if tw:
                        sy.wait_ge(sv.h, tw)
                    self.dma(sy, sin, self.wstage[:, :rowlen], win[nch * 128:(nch + 1) * 128, :])
                self.E("sync", ld)
                t_ld = sin.inc(16)
                self.emit_quant(self.wstage[:, :rowlen], self.wqnat[:, :rowlen], rowlen,
                                pre_waits=[(sin.h, t_ld), (st.h, last_tp)])
                def vq(v):
                    v.engine_nop().then_inc(sv.h, 1)
                self.E("vector", vq)
                quant_done = sv.inc()
                wt3 = wT.rearrange("p (kc f) -> p kc f", f=tcols)
                for b0 in range(0, nkc, 8):
                    nb = min(8, nkc - b0)
                    srcs = [self.wqnat[:, (b0 + j) * 128:(b0 + j + 1) * 128]
                            for j in range(nb)]
                    dst = wt3[:, b0:b0 + nb, nch * 128:(nch + 1) * 128]
                    self.emit_transpose_batch(srcs, dst,
                                              twaits=[(sv.h, quant_done), (ss.h, ss.c)])
                last_tp = st.c
            def sp(g, wT=wT, wT_hbm=wT_hbm, t=ss.c):
                g.wait_ge(ss.h, t)
                self.dma(g, sgp, wT_hbm, wT)
            self.E("gpsimd", sp)
            sgp.inc(16)

    def phase_ffn(self, src_hbm, dst_hbm, wq_wait=None):
        sin, sv, st, ss, sgp = self.sin, self.sv, self.st, self.ss, self.sgp
        resid_done = {}
        out_done = {}
        t_xqT = {}
        pend_tp = {}

        def emit_xq_tp(j):
            if j not in pend_tp or j in t_xqT:
                return
            srcs = [self.xq[j % 2][:, c * 128:(c + 1) * 128] for c in range(NKD)]
            tws = [(sv.h, pend_tp[j]), (ss.h, ss.c)]
            t_xqT[j] = self.emit_transpose_batch(srcs, self.xqT[j % 3], twaits=tws)
        silu_hist = {}
        hmul_hist = {}
        NJG = (FI + 511) // 512

        def stage1(i):
            b = i % 2
            bx = i % 4
            tw = resid_done.get(i - 4)
            def ld(sy, i=i, bx=bx, tw=tw):
                if tw is not None:
                    sy.wait_ge(sv.h, tw)
                self.dma(sy, sin, self.xt3[bx], src_hbm[i * 128:(i + 1) * 128, :])
            self.E("sync", ld)
            t_x = sin.inc(16)
            def vw(v, t=t_x, txe=t_xqT.get(i - 2)):
                v.wait_ge(sin.h, t)
                if txe is not None:
                    v.wait_ge(ss.h, txe)
            self.E("vector", vw)
            self.emit_ln(self.xt3[bx], self.xln)
            self.emit_quant(self.xln, self.xq[b], D)
            def vq(v):
                v.engine_nop().then_inc(sv.h, 1)
            self.E("vector", vq)
            pend_tp[i] = sv.inc()


        def stage2(i, mid_hook=None):
            b = i % 2
            bx = i % 4
            silu_done = {}
            hmul_done = {}
            hq_done = {}
            sp = silu_hist.get(i - 1, {})
            hp = hmul_hist.get(i - 1, {})

            def emit_gu(jn):
                w = min(512, FI - jn * 512)
                pb = jn % 2
                tws = silu_done.get(jn - 2, sp.get(jn + NJG - 2))
                twh = hmul_done.get(jn - 2, hp.get(jn + NJG - 2))
                def tgu(t, jn=jn, w=w, pb=pb, i=i, b=b, tws=tws, twh=twh):
                    t.wait_ge(ss.h, t_xqT[i])
                    if i == 0 and jn == 0 and wq_wait is not None:
                        t.wait_ge(sin.h, wq_wait)
                    if tws is not None:
                        t.wait_ge(ss.h, tws)
                    if twh is not None:
                        t.wait_ge(sv.h, twh)
                    last = None
                    for kc in range(NKD):
                        t.matmul(self.psg[pb][:, :w],
                                 self.xqT[i % 3][:, kc * 128:(kc + 1) * 128],
                                 self.wgT[:, kc * FI + jn * 512: kc * FI + jn * 512 + w],
                                 start=(kc == 0), stop=(kc == NKD - 1))
                        last = t.matmul(self.psu[pb][:, :w],
                                        self.xqT[i % 3][:, kc * 128:(kc + 1) * 128],
                                        self.wuT[:, kc * FI + jn * 512: kc * FI + jn * 512 + w],
                                        start=(kc == 0), stop=(kc == NKD - 1))
                    last.then_inc(st.h, 1)
                self.E("tensor", tgu)
                t_gu = st.inc()
                def se(s, jn=jn, w=w, pb=pb, t=t_gu):
                    s.wait_ge(st.h, t)
                    s.activation(out=self.sg_sb[:, jn * 512:jn * 512 + w],
                                 in_=self.psg[pb][:, :w], func=ACT.Silu).then_inc(ss.h, 1)
                self.E("scalar", se)
                silu_done[jn] = ss.inc()
                def vh(v, jn=jn, w=w, pb=pb, t=t_gu, tsil=silu_done[jn]):
                    v.wait_ge(st.h, t)
                    v.wait_ge(ss.h, tsil)
                    sl = slice(jn * 512, jn * 512 + w)
                    v.tensor_tensor(out=self.sg_sb[:, sl], in0=self.sg_sb[:, sl],
                                    in1=self.psu[pb][:, :w], op=ALU.mult).then_inc(sv.h, 1)
                self.E("vector", vh)
                hmul_done[jn] = sv.inc()
                self.emit_quant(self.sg_sb[:, jn * 512:jn * 512 + w],
                                self.hq[:, jn * 512:jn * 512 + w], w)
                def vq(v):
                    v.engine_nop().then_inc(sv.h, 1)
                self.E("vector", vq)
                hq_done[jn] = sv.inc()

            def emit_dn(jn):
                w = min(512, FI - jn * 512)
                nb = w // 128
                c0 = jn * 4
                srcs = [self.hq[:, (c0 + j) * 128:(c0 + j + 1) * 128] for j in range(nb)]
                dst = self.hqT[:, c0 * 128:(c0 + nb) * 128]
                t_hqT = self.emit_transpose_batch(srcs, dst,
                                                  twaits=[(sv.h, hq_done[jn]),
                                                          (ss.h, ss.c)])
                def td(t, jn=jn, nb=nb, c0=c0, t_hqT=t_hqT,
                       twp=(resid_done.get(i - 1) if jn == 0 else None)):
                    t.wait_ge(ss.h, t_hqT)
                    if twp is not None:
                        t.wait_ge(sv.h, twp)
                    last = None
                    for kc in range(c0, c0 + nb):
                        for j2 in range(2):
                            last = t.matmul(self.psd[j2][:, :512],
                                            self.hqT[:, kc * 128:(kc + 1) * 128],
                                            self.wdT[:, kc * D + j2 * 512: kc * D + j2 * 512 + 512],
                                            start=(kc == 0),
                                            stop=(kc == NKI - 1))
                    if jn == NJG - 1:
                        last.then_inc(st.h, 1)
                self.E("tensor", td)
                if jn == NJG - 1:
                    return st.inc()
                return None

            t_d = None
            for jn in range(NJG):
                emit_gu(jn)
                if jn == 1 and mid_hook is not None:
                    mid_hook()
                if jn == 3:
                    emit_xq_tp(i + 2)
                if jn >= 1:
                    t_d = emit_dn(jn - 1)
            t_d = emit_dn(NJG - 1)
            silu_hist[i] = silu_done
            hmul_hist[i] = hmul_done
            def vr(v, i=i, b=b, t=t_d, tw_out=out_done.get(i - 1)):
                v.wait_ge(st.h, t)
                if tw_out is not None:
                    v.wait_ge(sgp.h, tw_out)
                last = None
                for jn in range(2):
                    sl = slice(jn * 512, jn * 512 + 512)
                    last = v.scalar_tensor_tensor(out=self.x1t[:, sl], in0=self.psd[jn][:, :512],
                                                  scalar=0.5, in1=self.xt3[bx][:, sl],
                                                  op0=ALU.mult, op1=ALU.add)
                last.then_inc(sv.h, 1)
            self.E("vector", vr)
            resid_done[i] = sv.inc()
            def stx(g, i=i, t=resid_done[i]):
                g.wait_ge(sv.h, t)
                self.dma(g, sgp, dst_hbm[i * 128:(i + 1) * 128, :], self.x1t)
            self.E("gpsimd", stx)
            out_done[i] = sgp.inc(16)

        stage1(0)
        emit_xq_tp(0)
        stage1(1)
        emit_xq_tp(1)
        for i in range(NT):
            hook = (lambda j=i + 2: stage1(j)) if i + 2 < NT else None
            stage2(i, mid_hook=hook)

    def phase_b1(self):
        sin, sv, st, ss, sgp = self.sin, self.sv, self.st, self.ss, self.sgp
        def gw(g):
            self.dma(g, sgp, self.Wk_sb.rearrange("p (kc n) -> p kc n", n=D), self.Wk_in.rearrange("(kc p) n -> p kc n", p=128))
            self.dma(g, sgp, self.Wv_sb.rearrange("p (kc n) -> p kc n", n=D), self.Wv_in.rearrange("(kc p) n -> p kc n", p=128))
        self.E("gpsimd", gw)
        t_w = sgp.inc(32)
        prev_evac = {}
        for i in range(NT):
            b = i % 2
            def ld(sy, i=i, b=b, tw=prev_evac.get(i - 2)):
                if tw is not None:
                    sy.wait_ge(ss.h, tw)
                self.dma(sy, sin, self.xt[b], self.x1_hbm[i * 128:(i + 1) * 128, :])
            self.E("sync", ld)
            t_x = sin.inc(16)
            def vw(v, t=t_x):
                v.wait_ge(sin.h, t)
            self.E("vector", vw)
            self.emit_ln(self.xt[b], self.xln_bf, s_extra_waits=[(sin.h, t_x)],
                         sqscr=self.lnsq)
            def vq(v):
                v.engine_nop().then_inc(sv.h, 1)
            self.E("vector", vq)
            t_ln = sv.inc()
            srcs = [self.xln_bf[:, j * 128:(j + 1) * 128] for j in range(NKD)]
            tws = [(sv.h, t_ln), (ss.h, ss.c)]
            if i == 0:
                tws.append((sgp.h, t_w))
            t_xlnT = self.emit_transpose_batch(srcs, self.xlnT, twaits=tws)
            for half in range(2):
                pbank = self.psg[half]
                def tk(t, half=half, t_xlnT=t_xlnT, pbank=pbank,
                       twp=prev_evac.get(i - 1)):
                    t.wait_ge(ss.h, t_xlnT)
                    if twp is not None and half == 0:
                        t.wait_ge(ss.h, twp)
                    last = None
                    for kc in range(NKD):
                        for nl in range(4):
                            nc_ = half * 4 + nl
                            last = t.matmul(pbank[:, nl * 128:(nl + 1) * 128],
                                            self.Wk_sb[:, kc * D + nc_ * 128: kc * D + nc_ * 128 + 128],
                                            self.xlnT[:, kc * 128:(kc + 1) * 128],
                                            start=(kc == 0 and nl == 0),
                                            stop=(kc == NKD - 1 and nl == 3))
                    last.then_inc(st.h, 1)
                self.E("tensor", tk)
                t_k = st.inc()
                def sk(s, i=i, half=half, pbank=pbank, t=t_k):
                    s.wait_ge(st.h, t)
                    last = None
                    for nl in range(4):
                        nc_ = half * 4 + nl
                        last = s.activation(
                            out=self.k_sb[:, nc_ * S + i * 128: nc_ * S + i * 128 + 128],
                            in_=pbank[:, nl * 128:(nl + 1) * 128],
                            func=ACT.Relu, bias=self.bkp[:, nc_:nc_ + 1])
                    last.then_inc(ss.h, 1)
                self.E("scalar", sk)
                ss.inc()
            def tv(t, t_xlnT=t_xlnT, twp=prev_evac.get(i - 1)):
                if twp is not None:
                    t.wait_ge(ss.h, twp)
                for kc in range(NKD):
                    for jh in range(2):
                        t.matmul(self.psu[jh][:, :512],
                                 self.xlnT[:, kc * 128:(kc + 1) * 128],
                                 self.Wv_sb[:, kc * D + jh * 512: kc * D + jh * 512 + 512],
                                 start=(kc == 0), stop=False)
                last = None
                for jh in range(2):
                    last = t.matmul(self.psu[jh][:, :512], self.onesb,
                                    self.bvb[:, jh * 512:jh * 512 + 512],
                                    start=False, stop=True)
                last.then_inc(st.h, 1)
            self.E("tensor", tv)
            t_v = st.inc()
            def sv_(s, i=i, t=t_v):
                s.wait_ge(st.h, t)
                last = None
                for jh in range(2):
                    last = s.activation(
                        out=self.v_sb[:, i * D + jh * 512: i * D + jh * 512 + 512],
                        in_=self.psu[jh][:, :512], func=ACT.Relu)
                last.then_inc(ss.h, 1)
            self.E("scalar", sv_)
            prev_evac[i] = ss.inc()

    def phase_b2(self):
        sin, sv, st, ss, sgp = self.sin, self.sv, self.st, self.ss, self.sgp
        def gw(g):
            self.dma(g, sgp, self.Wq_sb.rearrange("p (kc n) -> p kc n", n=D),
                     self.Wq_in.rearrange("(kc p) n -> p kc n", p=128))
            self.dma(g, sgp, self.Wo_sb.rearrange("p (kc n) -> p kc n", n=D),
                     self.Wo_in.rearrange("(kc p) n -> p kc n", p=128))
        self.E("gpsimd", gw)
        t_w = sgp.inc(32)
        x2_out = {}
        t_qdone = {}
        t_x2v = {}
        t_yv = {}
        t_xlnTd = {}
        pend_ln = {}
        t_xd = {}
        pv_hist = {}

        def emit_xln_tp(j):
            if j not in pend_ln or j in t_xlnTd:
                return
            srcs = [self.xln_bf[:, c * 128:(c + 1) * 128] for c in range(NKD)]
            tws = [(sv.h, pend_ln[j]), (ss.h, ss.c)]
            if j == 0:
                tws.append((sgp.h, t_w))
            t_xlnTd[j] = self.emit_transpose_batch(srcs, self.xlnT, twaits=tws)

        def s_ld(i):
            b = i % 2
            tw = (t_w + 16 * (i - 1)) if i >= 2 else None
            def ld(sy, i=i, b=b, tw=tw):
                if tw is not None:
                    sy.wait_ge(sgp.h, tw)
                self.dma(sy, sin, self.xt[b], self.x1_hbm[i * 128:(i + 1) * 128, :])
            self.E("sync", ld)
            t_xd[i] = sin.inc(16)

        def qproj(i):
            b = i % 2
            for half in range(2):
                pbank = self.psg[half]
                def tq(t, half=half, b=b, t_xlnT=t_xlnTd[i], pbank=pbank,
                       twp=t_qdone.get(i - 1)):
                    t.wait_ge(ss.h, t_xlnT)
                    if twp is not None and half == 0:
                        t.wait_ge(ss.h, twp)
                    last = None
                    for kc in range(NKD):
                        for nl in range(4):
                            nc_ = half * 4 + nl
                            last = t.matmul(pbank[:, nl * 128:(nl + 1) * 128],
                                            self.Wq_sb[:, kc * D + nc_ * 128: kc * D + nc_ * 128 + 128],
                                            self.xlnT[:, kc * 128:(kc + 1) * 128],
                                            start=(kc == 0 and nl == 0),
                                            stop=(kc == NKD - 1 and nl == 3))
                    last.then_inc(st.h, 1)
                self.E("tensor", tq)
                t_q = st.inc()
                def sq(s, half=half, b=b, pbank=pbank, t=t_q):
                    s.wait_ge(st.h, t)
                    last = None
                    for nl in range(4):
                        nc_ = half * 4 + nl
                        last = s.activation(out=self.q_fm[:, nc_ * 128:(nc_ + 1) * 128],
                                            in_=pbank[:, nl * 128:(nl + 1) * 128],
                                            func=ACT.Relu, scale=1.0 / 32.0,
                                            bias=self.bqp32[:, nc_:nc_ + 1])
                    last.then_inc(ss.h, 1)
                self.E("scalar", sq)
                t_qdone[i] = ss.inc()

        qk_evacs = {}

        def emit_qk(i, g0_start, extra_evac_waits=(), extra_t_waits=()):
            b = i % 2
            W = (i + 1) * 128
            ngrp = (W + 511) // 512
            qk_evac = qk_evacs.setdefault(i, {})
            for g0 in range(g0_start, ngrp, 2):
                gpair = tuple(g for g in (g0, g0 + 1) if g < ngrp)
                if gpair[0] in qk_evac:
                    continue
                def tqk(t, gpair=gpair, W=W, tq_=t_qdone[i],
                        twp=qk_evac.get(g0 - 2),
                        twx=(t_x2v.get(i - 1) if g0 == 0 else None),
                        etw=(tuple(extra_t_waits) if g0 == g0_start else ())):
                    t.wait_ge(ss.h, tq_)
                    for h, c in etw:
                        t.wait_ge(h, c)
                    if twx is not None:
                        t.wait_ge(sv.h, twx)
                    if twp is not None:
                        t.wait_ge(sv.h, twp)
                    last = None
                    for kc in range(NKD):
                        for g_ in gpair:
                            wg_ = min(512, W - g_ * 512)
                            last = t.matmul(self.psu[g_ % 2][:, :wg_],
                                            self.q_fm[:, kc * 128:(kc + 1) * 128],
                                            self.k_sb[:, kc * S + g_ * 512: kc * S + g_ * 512 + wg_],
                                            start=(kc == 0), stop=(kc == NKD - 1))
                    last.then_inc(st.h, 1)
                self.E("tensor", tqk)
                t_qk = st.inc()
                def vqk(v, gpair=gpair, W=W, t=t_qk, ew=tuple(extra_evac_waits)):
                    v.wait_ge(st.h, t)
                    for h, c in ew:
                        v.wait_ge(h, c)
                    last = None
                    for g_ in gpair:
                        wg_ = min(512, W - g_ * 512)
                        last = v.tensor_copy(out=self.scores[:, g_ * 512:g_ * 512 + wg_],
                                             in_=self.psu[g_ % 2][:, :wg_])
                    last.then_inc(sv.h, 1)
                self.E("vector", vqk)
                t_e = sv.inc()
                for g_ in gpair:
                    qk_evac[g_] = t_e

        def s2(i):
            b = i % 2
            W = (i + 1) * 128
            ngrp = (W + 511) // 512
            t_sqv = None
            if i + 1 < NT:
                pw = [(sin.h, t_xd[i + 1])]
                if pv_hist.get(i - 1) is not None:
                    pw.append((st.h, pv_hist[i - 1]))
                t_sqv = self.emit_ln_stats(self.xt[(i + 1) % 2], self.attnT,
                                           pre_waits=pw)
            emit_qk(i, 2)
            qk_evac = qk_evacs[i]
            def vmask(v, i=i, W=W):
                v.tensor_tensor(out=self.scores[:, i * 128:W], in0=self.scores[:, i * 128:W],
                                in1=self.causal, op=ALU.subtract)
                v.tensor_reduce(out=self.negmax, in_=self.scores[:, 0:W], axis=AX.X,
                                op=ALU.max, negate=True)
                v.engine_nop().then_inc(sv.h, 1)
            self.E("vector", vmask)
            t_mask = sv.inc()
            if i + 1 < NT:
                self.emit_ln_norm(self.xt[(i + 1) % 2], self.xln_bf, t_sqv,
                                  pre_waits=[(ss.h, t_xlnTd[i])] if i in t_xlnTd else [])
                def vqn(v):
                    v.engine_nop().then_inc(sv.h, 1)
                self.E("vector", vqn)
                pend_ln[i + 1] = sv.inc()
            def sexp(s, W=W, t=t_mask):
                s.wait_ge(sv.h, t)
                s.activation(out=self.attn_bf[:, 0:W], in_=self.scores[:, 0:W],
                             func=ACT.Exp, bias=self.negmax,
                             accum_out=self.rowsum).then_inc(ss.h, 1)
            self.E("scalar", sexp)
            t_exp = ss.inc()
            if i + 1 < NT:
                emit_xln_tp(i + 1)
                qproj(i + 1)
                own = qk_evacs.get(i, {})
                t_own = max(own.values()) if own else 0
                emit_qk(i + 1, 0, extra_evac_waits=[(ss.h, t_exp)],
                        extra_t_waits=([(sv.h, t_own)] if t_own else []))
            def vrcp(v, t=t_exp):
                v.wait_ge(ss.h, t)
                v.reciprocal(out=self.rcpsum, in_=self.rowsum).then_inc(sv.h, 1)
            self.E("vector", vrcp)
            t_rcp = sv.inc()
            nkt = i + 1
            first = True
            pv_prev = None
            for b0 in range(0, nkt, 8):
                nb = min(8, nkt - b0)
                srcs = [self.attn_bf[:, (b0 + j) * 128:(b0 + j + 1) * 128] for j in range(nb)]
                sws = [] if pv_prev is None else [(st.h, pv_prev)]
                t_aT = self.emit_transpose_batch(srcs, self.attnT[:, :nb * 128],
                                                 twaits=[(ss.h, max(t_exp, ss.c)),
                                                         (sv.h, sv.c)],
                                                 swaits=sws)
                def tpv(t, b0=b0, nb=nb, t_aT=t_aT, first=first,
                        last_batch=(b0 + nb == nkt),
                        twy=(t_yv.get(i - 1) if first else None)):
                    t.wait_ge(ss.h, t_aT)
                    if twy is not None:
                        t.wait_ge(sv.h, twy)
                    last = None
                    for j in range(nb):
                        for jh in range(2):
                            last = t.matmul(self.psd[jh][:, :512],
                                            self.attnT[:, j * 128:(j + 1) * 128],
                                            self.v_sb[:, (b0 + j) * D + jh * 512:
                                                      (b0 + j) * D + jh * 512 + 512],
                                            start=(first and j == 0),
                                            stop=(last_batch and j == nb - 1))
                    last.then_inc(st.h, 1)
                self.E("tensor", tpv)
                pv_prev = st.inc()
                first = False
            pv_hist[i] = pv_prev
            def vy(v, t=pv_prev, trc=t_rcp):
                v.wait_ge(st.h, t)
                v.wait_ge(sv.h, trc)
                self.vgap(v)
                self.vgap(v)
                for jh in range(2):
                    v.tensor_scalar_mul(out=self.y_bf[:, jh * 512:(jh + 1) * 512],
                                        in0=self.psd[jh][:, :512], scalar1=self.rcpsum)
                v.engine_nop().then_inc(sv.h, 1)
            self.E("vector", vy)
            t_y = sv.inc()
            t_yv[i] = t_y
            srcs = [self.y_bf[:, j * 128:(j + 1) * 128] for j in range(NKD)]
            t_yT = self.emit_transpose_batch(srcs, self.yT,
                                             twaits=[(sv.h, t_y), (ss.h, ss.c)])
            hoisted = qk_evacs.get(i + 1, {})
            t_hoist = max(hoisted.values()) if hoisted else None
            def to(t, t_yT=t_yT, th=t_hoist):
                t.wait_ge(ss.h, t_yT)
                if th is not None:
                    t.wait_ge(sv.h, th)
                for kc in range(NKD):
                    for jh in range(2):
                        t.matmul(self.psu[jh][:, :512],
                                 self.yT[:, kc * 128:(kc + 1) * 128],
                                 self.Wo_sb[:, kc * D + jh * 512: kc * D + jh * 512 + 512],
                                 start=(kc == 0), stop=False)
                last = None
                for jh in range(2):
                    last = t.matmul(self.psu[jh][:, :512], self.onesb,
                                    self.bob[:, jh * 512:jh * 512 + 512],
                                    start=False, stop=True)
                last.then_inc(st.h, 1)
            self.E("tensor", to)
            t_o = st.inc()
            def vo(v, b=b, t=t_o):
                v.wait_ge(st.h, t)
                last = None
                for jh in range(2):
                    sl = slice(jh * 512, (jh + 1) * 512)
                    last = v.scalar_tensor_tensor(out=self.xt[b][:, sl],
                                                  in0=self.psu[jh][:, :512],
                                                  scalar=0.0, in1=self.xt[b][:, sl],
                                                  op0=ALU.max, op1=ALU.add)
                last.then_inc(sv.h, 1)
            self.E("vector", vo)
            t_x2 = sv.inc()
            t_x2v[i] = t_x2
            def stx(g, i=i, b=b, t=t_x2):
                g.wait_ge(sv.h, t)
                self.dma(g, sgp, self.x2_hbm[i * 128:(i + 1) * 128, :], self.xt[b])
            self.E("gpsimd", stx)
            x2_out[i] = sgp.inc(16)
            if i + 2 < NT:
                s_ld(i + 2)

        s_ld(0)
        s_ld(1)
        t_sqv0 = self.emit_ln_stats(self.xt[0], self.attnT,
                                    pre_waits=[(sin.h, t_xd[0])])
        self.emit_ln_norm(self.xt[0], self.xln_bf, t_sqv0)
        def vq0(v):
            v.engine_nop().then_inc(sv.h, 1)
        self.E("vector", vq0)
        pend_ln[0] = sv.inc()
        emit_xln_tp(0)
        qproj(0)
        emit_qk(0, 0)
        for i in range(NT):
            s2(i)


def _build_inputs(x_shard, ins):
    causal = (np.arange(128)[None, :] > np.arange(128)[:, None]).astype(np.float32) * 10000.0
    m = {
        "wg_in": ins["L0_ffn_w_gate"],
        "wu_in": ins["L0_ffn_w_up"],
        "wd_in": ins["L0_ffn_w_down"],
        "Wq_in": ins["L0_attn_Wq"],
        "Wk_in": ins["L0_attn_Wk"],
        "Wv_in": ins["L0_attn_Wv"],
        "Wo_in": ins["L0_attn_Wo"],
        "bq_in": ins["L0_attn_bq"],
        "bk_in": ins["L0_attn_bk"],
        "bv_in": ins["L0_attn_bv"],
        "bo_in": ins["L0_attn_bo"],
        "ident_in": np.eye(128, dtype=np.float32),
        "causal_in": causal,
        "ones_in": np.ones((1, 128), dtype=np.float32),
    }
    m["x_in"] = x_shard
    return m


def kernel(L0_attn_x, L0_attn_Wq, L0_attn_bq, L0_attn_Wk, L0_attn_bk,
           L0_attn_Wv, L0_attn_bv, L0_attn_Wo, L0_attn_bo,
           L0_ffn_w_gate, L0_ffn_w_up, L0_ffn_w_down):
    from concourse.bass_utils import run_bass_kernel_spmd
    ins = {k: np.ascontiguousarray(np.asarray(v, dtype=np.float32)) for k, v in {
        "L0_ffn_w_gate": L0_ffn_w_gate, "L0_ffn_w_up": L0_ffn_w_up,
        "L0_ffn_w_down": L0_ffn_w_down, "L0_attn_Wq": L0_attn_Wq,
        "L0_attn_Wk": L0_attn_Wk, "L0_attn_Wv": L0_attn_Wv,
        "L0_attn_Wo": L0_attn_Wo, "L0_attn_bq": L0_attn_bq,
        "L0_attn_bk": L0_attn_bk, "L0_attn_bv": L0_attn_bv,
        "L0_attn_bo": L0_attn_bo}.items()}
    x = np.ascontiguousarray(np.asarray(L0_attn_x, dtype=np.float32))
    kb = KB(dev_outputs=False)
    nc = kb.build(phases=("P", "A", "B1", "B2", "C"))
    mybir.codegen_inst_isa_subclasses(nc)
    in_maps = [_build_inputs(np.ascontiguousarray(x[c]), ins) for c in range(8)]
    res = run_bass_kernel_spmd(nc, in_maps, core_ids=list(range(8)))
    return np.stack([np.asarray(r["out"], dtype=np.float32) for r in res.results], axis=0)


# version 17 (final)
# speedup vs baseline: 1.0380x; 1.0371x over previous
import concourse.bass_utils as _bu
if not getattr(_bu, "_ldw_patched", False):
    _orig_gwa = _bu.get_walrus_args
    def _gwa(*a, **k):
        return [x.replace("--enable-ldw-opt=false", "--enable-ldw-opt=true")
                for x in _orig_gwa(*a, **k)]
    _bu.get_walrus_args = _gwa
    _bu._ldw_patched = True
import numpy as np
import concourse.dve_ops as dve_ops
from concourse.dve_op import DveOp
from concourse.dve_spec import Spec, Src0, Src1, C0, C1, C2, C3, Zero, Bin, select
from concourse.dve_spec import lower as spec_lower
from concourse.dve_uop import AluOp
from concourse.dve_table_gen import dve_ver_for

_VER = dve_ver_for("TRN2")


def _f(x):
    return np.float64 if getattr(x, "dtype", None) == np.float64 else np.float32


def _mk(name, body, ref, perf_en=False):
    spec = Spec(body=body)
    sha = spec_lower(spec, ver=_VER, perf_en=perf_en, name=name).sha(_VER)
    return DveOp(name, spec, subdim=False, uops_sha={_VER: sha}, reference=ref,
                 perf_en={_VER: perf_en} if perf_en else {})


_t = Bin(AluOp.MULTIPLY, Src0, Src1)
QABS = _mk("QABS_ANT", Bin(AluOp.MAX, _t, Bin(AluOp.SUBTRACT, Zero, _t)),
           lambda in0, in1, s0, s1, imm2: np.abs(in0.astype(_f(in0)) * in1))

_lo = (Bin(AluOp.IS_GT, Src0, C0) + Bin(AluOp.IS_GT, Src0, C1)) + \
      (Bin(AluOp.IS_GT, Src0, C2) + Bin(AluOp.IS_GT, Src0, C3))


def _qlo_ref(in0, in1, s0, s1, imm2):
    f = _f(in0)
    a = in0.astype(f)
    c3 = np.asarray(in1, f).reshape(-1, 1)
    while c3.ndim < a.ndim:
        c3 = c3[..., None]
    return ((a > s0).astype(f) + (a > s1) + (a > imm2) + (a > c3)).astype(f)


QLO = _mk("QLO_ANT", _lo, _qlo_ref)

_c6 = Bin(AluOp.IS_GT, Src0, C2)
_hi = ((Bin(AluOp.IS_GT, Src0, C0) + Bin(AluOp.IS_GT, Src0, C1)) + _c6) + _c6


def _qhi_ref(in0, in1, s0, s1, imm2):
    f = _f(in0)
    a = in0.astype(f)
    return ((a > s0).astype(f) + (a > s1) + 2.0 * (a > imm2)).astype(f)


QHI = _mk("QHI_ANT", _hi, _qhi_ref)

_cmb = (Src0 * C0 + C1) + Src1


def _qcmb_ref(in0, in1, s0, s1, imm2):
    f = _f(in0)
    return (in0.astype(f) * s0 + s1) + in1


QCMB = _mk("QCMB_ANT", _cmb, _qcmb_ref, perf_en=True)

_sg = select(Bin(AluOp.IS_LT, Src1, Zero), Bin(AluOp.SUBTRACT, Zero, Src0), Src0)


def _qsgn_ref(in0, in1, s0, s1, imm2):
    f = _f(in0)
    a = in0.astype(f)
    return np.where(in1 < 0, -a, a).astype(f)


QSGN = _mk("QSGN_ANT", _sg, _qsgn_ref)

_VICTIMS = ("DVEHASH32L_SEED", "DVEHASH32L_BODY", "DVEHASH32L_DRAIN",
            "BITWISE_XOR_TT_BV", "SHIFT_RIGHT_TT_BV")
_NEW = (QABS, QLO, QHI, QCMB, QSGN)


def register():
    if QABS.name in dve_ops._SUB_OPCODE_FOR_NAME:
        return
    for victim, op in zip(_VICTIMS, _NEW):
        idx = next(i for i, o in enumerate(dve_ops.OPS) if o.name == victim)
        row = dve_ops._SUB_OPCODE_FOR_NAME.pop(victim)
        dve_ops.OPS[idx] = op
        dve_ops._SUB_OPCODE_FOR_NAME[op.name] = row
        dve_ops.CUSTOM_DVE_SPECS[op.name] = op.spec
        dve_ops.CUSTOM_DVE_REFS[op.name] = op.reference


import numpy as np
from contextlib import ExitStack
import concourse.bass as bass
import concourse.mybir as mybir

F32 = mybir.dt.float32
BF16 = mybir.dt.bfloat16
ALU = mybir.AluOpType
ACT = mybir.ActivationFunctionType
AX = mybir.AxisListType

S = 4096
D = 1024
FI = 2816
NT = S // 128
NKD = D // 128
NKI = FI // 128
MIDS = (0.25, 0.75, 1.25, 1.75, 2.5, 3.5, 5.0)
DELTAS = (0.5, 0.5, 0.5, 0.5, 1.0, 1.0, 2.0)
QCHUNK = 704

ENGINES = ("sync", "gpsimd", "vector", "scalar", "tensor")

_register_qops = register


class Sem:
    __slots__ = ("h", "c")
    def __init__(self, h):
        self.h = h
        self.c = 0
    def inc(self, n=1):
        self.c += n
        return self.c


class KB:
    def __init__(self, dev_outputs=False):
        self.dev = dev_outputs
        self.nc = bass.Bass("TRN2", target_bir_lowering=False, debug=False,
                            detect_race_conditions=False)
        self.es = ExitStack()
        self.ops = {e: [] for e in ENGINES}
        self._uid = 0
        self._dmac = {}
        self._sect = {}

    def uid(self):
        self._uid += 1
        return str(self._uid)

    def E(self, eng, fn):
        self.ops[eng].append(fn)

    def dma(self, eng, sem, out, in_, **kw):
        wv = self._dmac.get(id(sem), 0)
        self._dmac[id(sem)] = wv + 16
        if wv:
            eng.wait_ge(sem.h, wv)
        eng.dma_start(out=out, in_=in_, **kw).then_inc(sem.h, 16)

    def vgap(self, v):
        v.tensor_scalar(out=self.vgapb, in0=self.vgapb, scalar1=1.0,
                        scalar2=None, op0=ALU.mult)

    def tap(self, name, ap, waits=()):
        if not self.dev:
            return None
        t = self.nc.dram_tensor("tap_" + name, list(ap.shape), ap.dtype,
                                kind="ExternalOutput").ap()
        def g(gp, t=t, ap=ap, waits=tuple(waits)):
            for h, c in waits:
                gp.wait_ge(h, c)
            self.dma(gp, self.sgp, t, ap)
        self.E("gpsimd", g)
        return self.sgp.inc(16)

    def flush(self):
        for eng in ENGINES:
            lst = self.ops[eng]
            if not lst:
                continue
            head = None
            tail = None
            if eng == "vector":
                prev = self._sect.get(eng)
                if prev is not None:
                    head = lambda v, c=prev: v.wait_ge(self.sv.h, c)
                tail = lambda v: v.engine_nop().then_inc(self.sv.h, 1)
                self._sect[eng] = self.sv.inc()
            elif eng == "scalar":
                prev = self._sect.get(eng)
                if prev is not None:
                    head = lambda s, c=prev: s.wait_ge(self.ss.h, c)
                tail = lambda s: s.copy(out=self.sgapb,
                                        in_=self.epsv).then_inc(self.ss.h, 1)
                self._sect[eng] = self.ss.inc()
            elif eng == "tensor":
                prev = self._sect.get(eng)
                if prev is not None:
                    head = lambda t, c=prev: t.wait_ge(self.st.h, c)
                sev = self.ss.c
                def tail(t, sev=sev):
                    t.wait_ge(self.ss.h, sev)
                    t.transpose(self.psT[:, 0:128], self.ident,
                                self.ident).then_inc(self.st.h, 1)
                self._sect[eng] = self.st.inc()
            def runner(e, lst=lst, head=head, tail=tail):
                if head is not None:
                    head(e)
                for f in lst:
                    f(e)
                if tail is not None:
                    tail(e)
            getattr(self.block, eng)(runner)
        self.ops = {e: [] for e in ENGINES}

    def barrier(self):
        self.flush()
        snaps = [(s.h, s.c) for s in (self.sin, self.sgp, self.sv, self.ss, self.st) if s.c > 0]
        for eng in ENGINES:
            def w(e, snaps=tuple(snaps)):
                for h, c in snaps:
                    e.wait_ge(h, c)
            self.E(eng, w)

    def build(self, phases=("P", "A", "B1", "B2", "C")):
        _register_qops()
        nc = self.nc
        T = lambda n, shp, dt, kind: nc.dram_tensor(n, shp, dt, kind=kind).ap()
        self.x_in = T("x_in", [S, D], F32, "ExternalInput")
        self.wg_in = T("wg_in", [FI, D], F32, "ExternalInput")
        self.wu_in = T("wu_in", [FI, D], F32, "ExternalInput")
        self.wd_in = T("wd_in", [D, FI], F32, "ExternalInput")
        self.Wq_in = T("Wq_in", [D, D], F32, "ExternalInput")
        self.Wk_in = T("Wk_in", [D, D], F32, "ExternalInput")
        self.Wv_in = T("Wv_in", [D, D], F32, "ExternalInput")
        self.Wo_in = T("Wo_in", [D, D], F32, "ExternalInput")
        self.bq_in = T("bq_in", [D], F32, "ExternalInput")
        self.bk_in = T("bk_in", [D], F32, "ExternalInput")
        self.bv_in = T("bv_in", [D], F32, "ExternalInput")
        self.bo_in = T("bo_in", [D], F32, "ExternalInput")
        self.ident_in = T("ident_in", [128, 128], F32, "ExternalInput")
        self.causal_in = T("causal_in", [128, 128], F32, "ExternalInput")
        self.ones_in = T("ones_in", [1, 128], F32, "ExternalInput")
        sk = "ExternalOutput" if self.dev else "Internal"
        self.x1_hbm = T("x1_hbm", [S, D], F32, sk)
        self.x2_hbm = T("x2_hbm", [S, D], F32, sk)
        self.wgT_hbm = T("wgT_hbm", [128, NKD * FI], BF16, "Internal")
        self.wuT_hbm = T("wuT_hbm", [128, NKD * FI], BF16, "Internal")
        self.wdT_hbm = T("wdT_hbm", [128, NKI * D], BF16, "Internal")
        self.out = T("out", [S, D], F32, "ExternalOutput")

        es = self.es
        sb = lambda n, shp, dt: es.enter_context(nc.sbuf_tensor(n, shp, dt))[:]
        psf = lambda n, shp, dt: es.enter_context(nc.psum_tensor(n, shp, dt))[:]
        sem = lambda n: Sem(es.enter_context(nc.semaphore(n)))

        self.sin = sem("sin")
        self.sgp = sem("sgp")
        self.sv = sem("sv")
        self.ss = sem("ss")
        self.st = sem("st")

        self.ident = sb("ident", [128, 128], BF16)
        self.causal = sb("causal", [128, 128], F32)
        self.onesb = sb("onesb", [1, 128], BF16)
        self.bvb = sb("bvb", [1, D], BF16)
        self.bob = sb("bob", [1, D], BF16)
        self.bkp = sb("bkp", [128, NKD], F32)
        self.bqp32 = sb("bqp32", [128, NKD], F32)
        self.xt = [sb("xt0", [128, D], F32), sb("xt1", [128, D], F32)]
        self.lnb = {k: sb("ln_" + k, [128, 1], F32)
                    for k in ("sm", "m", "ssum", "msq", "var", "sqv", "rstd")}
        self.epsv = sb("epsv", [128, 1], F32)
        self.c175 = sb("c175", [128, 1], F32)
        self.vgapb = sb("vgapb", [128, 32], F32)
        self.sgapb = sb("sgapb", [128, 1], F32)
        self.psT = psf("psT", [128, 8 * 128], BF16)
        self.psg = [psf("psg0", [128, 512], F32), psf("psg1", [128, 512], F32)]
        self.psu = [psf("psu0", [128, 512], F32), psf("psu1", [128, 512], F32)]
        self.psd = [psf("psd0", [128, 512], F32), psf("psd1", [128, 512], F32)]

        self.block = es.enter_context(nc.Block())
        sin, sgp, sv = self.sin, self.sgp, self.sv

        def g0(g):
            self.dma(g, sgp, self.ident, self.ident_in)
            self.dma(g, sgp, self.onesb, self.ones_in)
            self.dma(g, sgp, self.bvb, self.bv_in.unsqueeze(0))
            self.dma(g, sgp, self.bob, self.bo_in.unsqueeze(0))
        self.E("gpsimd", g0)
        sgp.inc(64)
        self._ident_ready = 16
        def s0(sy):
            self.dma(sy, sin, self.causal, self.causal_in)
            self.dma(sy, sin, self.bkp, self.bk_in.rearrange("(c p) -> p c", p=128),
                         allow_slow_non_contiguous=True)
            self.dma(sy, sin, self.bqp32, self.bq_in.rearrange("(c p) -> p c", p=128),
                         allow_slow_non_contiguous=True)
        self.E("sync", s0)
        sin.inc(48)
        t_bq = sin.c
        def v0(v, t=t_bq):
            v.wait_ge(sin.h, t)
            z1 = self.bqp32[:, 0:1]
            v.tensor_scalar(out=self.epsv, in0=z1, scalar1=0.0, scalar2=1e-5,
                            op0=ALU.mult, op1=ALU.add)
            v.tensor_scalar(out=self.c175, in0=z1, scalar1=0.0, scalar2=1.75,
                            op0=ALU.mult, op1=ALU.add)
            v.tensor_scalar(out=self.vgapb, in0=z1.broadcast_to([128, 32]),
                            scalar1=0.0, scalar2=0.0, op0=ALU.mult, op1=ALU.add)
            v.tensor_scalar(out=self.bqp32, in0=self.bqp32, scalar1=1.0 / 32.0,
                            scalar2=None, op0=ALU.mult).then_inc(sv.h, 1)
        self.E("vector", v0)
        sv.inc()

        if "P" in phases or "A" in phases:
            with ExitStack() as pes:
                self.ffn_wbufs(pes)
                if "P" in phases:
                    with ExitStack() as pro_es:
                        u2 = self.uid()
                        self.wstage = [pro_es.enter_context(
                            nc.sbuf_tensor("wstage%s_%d" % (u2, k), [128, FI], F32))[:]
                            for k in range(2)]
                        self.wqnat = [pro_es.enter_context(
                            nc.sbuf_tensor("wqnat%s_%d" % (u2, k), [128, FI], BF16))[:]
                            for k in range(2)]
                        self.phase_prologue()
                        self.barrier()
                if "A" in phases:
                    self.ffn_tilebufs(pes)
                    self.phase_ffn(self.x_in, self.x1_hbm)
                    self.barrier()
        if "B1" in phases or "B2" in phases:
            with ExitStack() as pes:
                sbp = lambda n, shp, dt: pes.enter_context(nc.sbuf_tensor(n, shp, dt))[:]
                self.k_sb = sbp("k_sb", [128, NKD * S], BF16)
                self.v_sb = sbp("v_sb", [128, NT * D], BF16)
                self.xln_bf = sbp("xln_bf", [128, D], BF16)
                self.xlnT = sbp("xlnT", [128, NKD * 128], BF16)
                if "B1" in phases:
                    with ExitStack() as b1es:
                        sb1 = lambda n, shp, dt: b1es.enter_context(nc.sbuf_tensor(n, shp, dt))[:]
                        self.Wk_sb = sb1("Wk_sb", [128, NKD * D], BF16)
                        self.Wv_sb = sb1("Wv_sb", [128, NKD * D], BF16)
                        self.lnsq = sb1("lnsq_b", [128, D], F32)
                        self.phase_b1()
                        self.barrier()
                if "B2" in phases:
                    with ExitStack() as b2es:
                        sb2 = lambda n, shp, dt: b2es.enter_context(nc.sbuf_tensor(n, shp, dt))[:]
                        self.Wq_sb = sb2("Wq_sb", [128, NKD * D], BF16)
                        self.Wo_sb = sb2("Wo_sb", [128, NKD * D], BF16)
                        self.scores = sb2("scores", [128, S], F32)
                        self.attn_bf = sb2("attn_bf", [128, S], BF16)
                        self.q_fm = sb2("q_fm", [128, NKD * 128], BF16)

                        self.rowsum = sb2("rowsum", [128, 1], F32)
                        self.negmax = sb2("negmax", [128, 1], F32)
                        self.rcpsum = sb2("rcpsum", [128, 1], F32)
                        self.attnT = sb2("attnT", [128, 8 * 128], BF16)
                        self.y_bf = self.attn_bf[:, 0:D]
                        self.yT = self.attn_bf[:, D:2 * D]
                        self.phase_b2()
                        self.barrier()
        if "C" in phases:
            with ExitStack() as pes:
                self.ffn_wbufs(pes)
                self.ffn_tilebufs(pes)
                def ldw(sy):
                    self.dma(sy, self.sin, self.wgT, self.wgT_hbm)
                    self.dma(sy, self.sin, self.wuT, self.wuT_hbm)
                    self.dma(sy, self.sin, self.wdT, self.wdT_hbm)
                self.E("sync", ldw)
                self.sin.inc(48)
                wq_ready = self.sin.c
                self.phase_ffn(self.x2_hbm, self.out, wq_wait=wq_ready)
                self.flush()
        def fin(g):
            g.wait_ge(self.sgp.h, self.sgp.c)
        self.E("gpsimd", fin)
        self.flush()
        self.es.close()
        return nc

    def ffn_wbufs(self, pes):
        nc = self.nc
        u = self.uid()
        sbp = lambda n, shp, dt: pes.enter_context(nc.sbuf_tensor(n + u, shp, dt))[:]
        self.wgT = sbp("wgT", [128, NKD * FI], BF16)
        self.wuT = sbp("wuT", [128, NKD * FI], BF16)
        self.wdT = sbp("wdT", [128, NKI * D], BF16)
        self.q = {
            "absz": sbp("q_absz", [128, QCHUNK], F32),
            "an": sbp("q_an", [128, QCHUNK], F32),
            "ca": sbp("q_ca", [128, QCHUNK], BF16),
            "cb": sbp("q_cb", [128, QCHUNK], BF16),
            "gm": sbp("q_gm", [128, QCHUNK // 16], F32),
            "sc": sbp("q_sc", [128, QCHUNK // 16], F32),
            "rcp": sbp("q_rcp", [128, QCHUNK // 16], F32),
            "sg16": sbp("q_sg16", [128, QCHUNK], BF16),
        }

    def ffn_tilebufs(self, pes):
        nc = self.nc
        u = self.uid()
        sbp = lambda n, shp, dt: pes.enter_context(nc.sbuf_tensor(n + u, shp, dt))[:]
        self.xln = sbp("xln", [128, D], F32)
        self.xq = [sbp("xq0", [128, D], BF16), sbp("xq1", [128, D], BF16)]
        self.xt3 = [self.xt[0], self.xt[1], sbp("xt2", [128, D], F32),
                    sbp("xt3", [128, D], F32)]
        self.xqT = [sbp("xqT0", [128, NKD * 128], BF16),
                    sbp("xqT1", [128, NKD * 128], BF16),
                    sbp("xqT2", [128, NKD * 128], BF16)]
        self.sg_sb = sbp("sg_sb", [128, FI], F32)
        self.hq = sbp("hq", [128, FI], BF16)
        self.hqT = sbp("hqT", [128, NKI * 128], BF16)
        self.x1t = sbp("x1t", [128, D], F32)

    def emit_quant(self, z_ap, out_ap, width, pre_waits=()):
        q = self.q
        nchunks = (width + QCHUNK - 1) // QCHUNK
        for c in range(nchunks):
            w = min(QCHUNK, width - c * QCHUNK)
            g = w // 16
            z = z_ap[:, c * QCHUNK:c * QCHUNK + w]
            o = out_ap[:, c * QCHUNK:c * QCHUNK + w]
            pw = tuple(pre_waits) if c == 0 else ()
            def f(v, z=z, o=o, w=w, g=g, pw=pw):
                for h, cnt in pw:
                    v.wait_ge(h, cnt)
                an = q["an"][:, :w]; qa = q["absz"][:, :w]
                ca = q["ca"][:, :w]; cb = q["cb"][:, :w]; sg16 = q["sg16"][:, :w]
                gm = q["gm"][:, :g]; sc = q["sc"][:, :g]; rcp = q["rcp"][:, :g]
                v.tensor_reduce(out=gm, in_=z.rearrange("p (g k) -> p g k", k=16),
                                axis=AX.X, op=ALU.max, apply_absolute_value=True)
                self.vgap(v)
                v.tensor_scalar(out=sc, in0=gm, scalar1=1.0 / 6.0, scalar2=1e-12,
                                op0=ALU.mult, op1=ALU.max)
                self.vgap(v)
                v.reciprocal_approx_fast(out=rcp, in_=sc)
                self.vgap(v)
                rb = rcp.unsqueeze(2).broadcast_to([128, g, 16])
                v._custom_dve(QABS, out=an.rearrange("p (g k) -> p g k", k=16),
                              in0=z.rearrange("p (g k) -> p g k", k=16), in1=rb)
                v._custom_dve(QLO, out=ca, in0=an, in1=self.c175,
                              s0=MIDS[0], s1=MIDS[1], imm2=MIDS[2])
                v._custom_dve(QHI, out=cb, in0=an,
                              s0=MIDS[4], s1=MIDS[5], imm2=MIDS[6])
                v._custom_dve(QCMB, out=sg16, in0=ca, in1=cb, s0=0.5, s1=0.0)
                sbc = sc.unsqueeze(2).broadcast_to([128, g, 16])
                v.tensor_tensor(out=qa.rearrange("p (g k) -> p g k", k=16),
                                in0=sg16.rearrange("p (g k) -> p g k", k=16),
                                in1=sbc, op=ALU.mult)
                v._custom_dve(QSGN, out=o, in0=qa, in1=z)
            self.E("vector", f)

    def emit_ln_stats(self, xt, sqscr, pre_waits=()):
        lnb = self.lnb
        sv = self.sv
        def v1(v, pw=tuple(pre_waits)):
            for h, c in pw:
                v.wait_ge(h, c)
            v.reduce_sum(out=lnb["sm"], in_=xt, axis=AX.X)
            v.scalar_tensor_tensor(out=sqscr, in0=xt, scalar=1.0, in1=xt,
                                   op0=ALU.mult, op1=ALU.mult,
                                   accum_out=lnb["ssum"])
            v.tensor_scalar(out=lnb["m"], in0=lnb["sm"], scalar1=1.0 / D,
                            scalar2=None, op0=ALU.mult)
            v.tensor_scalar(out=lnb["var"], in0=lnb["ssum"], scalar1=1.0 / D,
                            scalar2=None, op0=ALU.mult)
            v.tensor_tensor(out=lnb["msq"], in0=lnb["m"], in1=lnb["m"], op=ALU.mult)
            self.vgap(v)
            self.vgap(v)
            v.tensor_tensor(out=lnb["var"], in0=lnb["var"], in1=lnb["msq"],
                            op=ALU.subtract).then_inc(sv.h, 1)
        self.E("vector", v1)
        t_var = sv.inc()
        def s2(s, t=t_var):
            s.wait_ge(sv.h, t)
            s.activation(out=lnb["sqv"], in_=lnb["var"], func=ACT.Sqrt,
                         bias=self.epsv).then_inc(self.ss.h, 1)
        self.E("scalar", s2)
        return self.ss.inc()

    def emit_ln_norm(self, xt, xln_out, t_sqv, pre_waits=()):
        lnb = self.lnb
        def v3(v, t=t_sqv, pw=tuple(pre_waits)):
            v.wait_ge(self.ss.h, t)
            for h, c in pw:
                v.wait_ge(h, c)
            v.reciprocal(out=lnb["rstd"], in_=lnb["sqv"])
            self.vgap(v)
            self.vgap(v)
            v.tensor_scalar(out=xln_out, in0=xt, scalar1=lnb["m"],
                            scalar2=lnb["rstd"], op0=ALU.subtract, op1=ALU.mult)
        self.E("vector", v3)

    def emit_ln(self, xt, xln_out, s_extra_waits=(), sqscr=None):
        lnb = self.lnb
        sv, ss = self.sv, self.ss
        if sqscr is None:
            sqscr = xln_out
        def v1(v, sqscr=sqscr):
            v.reduce_sum(out=lnb["sm"], in_=xt, axis=AX.X)
            v.scalar_tensor_tensor(out=sqscr, in0=xt, scalar=1.0, in1=xt,
                                   op0=ALU.mult, op1=ALU.mult,
                                   accum_out=lnb["ssum"])
            v.tensor_scalar(out=lnb["m"], in0=lnb["sm"], scalar1=1.0 / D,
                            scalar2=None, op0=ALU.mult)
            v.tensor_scalar(out=lnb["var"], in0=lnb["ssum"], scalar1=1.0 / D,
                            scalar2=None, op0=ALU.mult)
            v.tensor_tensor(out=lnb["msq"], in0=lnb["m"], in1=lnb["m"], op=ALU.mult)
            self.vgap(v)
            self.vgap(v)
            v.tensor_tensor(out=lnb["var"], in0=lnb["var"], in1=lnb["msq"],
                            op=ALU.subtract).then_inc(sv.h, 1)
        self.E("vector", v1)
        t_var = sv.inc()
        def s2(s, t=t_var, sw=tuple(s_extra_waits)):
            for h, c in sw:
                s.wait_ge(h, c)
            s.wait_ge(sv.h, t)
            s.activation(out=lnb["sqv"], in_=lnb["var"], func=ACT.Sqrt,
                         bias=self.epsv).then_inc(ss.h, 1)
        self.E("scalar", s2)
        t_sqv = ss.inc()
        def v3(v, t=t_sqv):
            v.wait_ge(ss.h, t)
            v.reciprocal(out=lnb["rstd"], in_=lnb["sqv"])
            self.vgap(v)
            self.vgap(v)
            v.tensor_scalar(out=xln_out, in0=xt, scalar1=lnb["m"],
                            scalar2=lnb["rstd"], op0=ALU.subtract, op1=ALU.mult)
        self.E("vector", v3)

    def emit_transpose_batch(self, srcs, dst_ap, twaits=(), swaits=()):
        st, ss = self.st, self.ss
        nb = len(srcs)
        if self._ident_ready is not None:
            twaits = list(twaits) + [(self.sgp.h, self._ident_ready)]
            self._ident_ready = None
        def t1(t, srcs=tuple(srcs), twaits=tuple(twaits)):
            for h, c in twaits:
                t.wait_ge(h, c)
            last = None
            for j, s in enumerate(srcs):
                last = t.transpose(self.psT[:, j * 128:(j + 1) * 128], s, self.ident)
            last.then_inc(st.h, 1)
        self.E("tensor", t1)
        t_tp = st.inc()
        def s1(s, dst_ap=dst_ap, nb=nb, t=t_tp, swaits=tuple(swaits)):
            s.wait_ge(st.h, t)
            for h, c in swaits:
                s.wait_ge(h, c)
            src = self.psT[:, :nb * 128]
            if dst_ap.ndim == 3:
                src = src.rearrange("p (j f) -> p j f", f=128)
            s.copy(out=dst_ap, in_=src).then_inc(ss.h, 1)
        self.E("scalar", s1)
        return ss.inc()

    def phase_prologue(self):
        sin, sv, st, ss, sgp = self.sin, self.sv, self.st, self.ss, self.sgp
        jobs = [(self.wg_in, NKI, NKD, FI, self.wgT, self.wgT_hbm),
                (self.wu_in, NKI, NKD, FI, self.wuT, self.wuT_hbm),
                (self.wd_in, NKD, NKI, D, self.wdT, self.wdT_hbm)]
        quant_done = {}
        tp_done = {}
        ci = 0
        for (win, nchunks, nkc, tcols, wT, wT_hbm) in jobs:
            rowlen = win.shape[1]
            for nch in range(nchunks):
                wb = ci % 2
                def ld(sy, win=win, nch=nch, rowlen=rowlen, wb=wb,
                       tw=quant_done.get(ci - 2)):
                    if tw is not None:
                        sy.wait_ge(sv.h, tw)
                    self.dma(sy, sin, self.wstage[wb][:, :rowlen],
                             win[nch * 128:(nch + 1) * 128, :])
                self.E("sync", ld)
                t_ld = sin.inc(16)
                pre = [(sin.h, t_ld)]
                if ci - 2 in tp_done:
                    pre.append((st.h, tp_done[ci - 2]))
                self.emit_quant(self.wstage[wb][:, :rowlen], self.wqnat[wb][:, :rowlen],
                                rowlen, pre_waits=pre)
                def vq(v):
                    v.engine_nop().then_inc(sv.h, 1)
                self.E("vector", vq)
                quant_done[ci] = sv.inc()
                wt3 = wT.rearrange("p (kc f) -> p kc f", f=tcols)
                for b0 in range(0, nkc, 8):
                    nb = min(8, nkc - b0)
                    srcs = [self.wqnat[wb][:, (b0 + j) * 128:(b0 + j + 1) * 128]
                            for j in range(nb)]
                    dst = wt3[:, b0:b0 + nb, nch * 128:(nch + 1) * 128]
                    self.emit_transpose_batch(srcs, dst,
                                              twaits=[(sv.h, quant_done[ci]), (ss.h, ss.c)])
                tp_done[ci] = st.c
                ci += 1
            def sp(g, wT=wT, wT_hbm=wT_hbm, t=ss.c):
                g.wait_ge(ss.h, t)
                self.dma(g, sgp, wT_hbm, wT)
            self.E("gpsimd", sp)
            sgp.inc(16)

    def phase_ffn(self, src_hbm, dst_hbm, wq_wait=None):
        sin, sv, st, ss, sgp = self.sin, self.sv, self.st, self.ss, self.sgp
        resid_done = {}
        out_done = {}
        t_xqT = {}
        pend_tp = {}

        def emit_xq_tp(j):
            if j not in pend_tp or j in t_xqT:
                return
            srcs = [self.xq[j % 2][:, c * 128:(c + 1) * 128] for c in range(NKD)]
            tws = [(sv.h, pend_tp[j]), (ss.h, ss.c)]
            t_xqT[j] = self.emit_transpose_batch(srcs, self.xqT[j % 3], twaits=tws)
        silu_hist = {}
        hmul_hist = {}
        NJG = (FI + 511) // 512

        def stage1(i):
            b = i % 2
            bx = i % 4
            tw = resid_done.get(i - 4)
            def ld(sy, i=i, bx=bx, tw=tw):
                if tw is not None:
                    sy.wait_ge(sv.h, tw)
                self.dma(sy, sin, self.xt3[bx], src_hbm[i * 128:(i + 1) * 128, :])
            self.E("sync", ld)
            t_x = sin.inc(16)
            def vw(v, t=t_x, txe=t_xqT.get(i - 2)):
                v.wait_ge(sin.h, t)
                if txe is not None:
                    v.wait_ge(ss.h, txe)
            self.E("vector", vw)
            self.emit_ln(self.xt3[bx], self.xln)
            self.emit_quant(self.xln, self.xq[b], D)
            def vq(v):
                v.engine_nop().then_inc(sv.h, 1)
            self.E("vector", vq)
            pend_tp[i] = sv.inc()


        def stage2(i, mid_hook=None):
            b = i % 2
            bx = i % 4
            silu_done = {}
            hmul_done = {}
            hq_done = {}
            sp = silu_hist.get(i - 1, {})
            hp = hmul_hist.get(i - 1, {})

            def emit_gu(jn):
                w = min(512, FI - jn * 512)
                pb = jn % 2
                tws = silu_done.get(jn - 2, sp.get(jn + NJG - 2))
                twh = hmul_done.get(jn - 2, hp.get(jn + NJG - 2))
                def tgu(t, jn=jn, w=w, pb=pb, i=i, b=b, tws=tws, twh=twh):
                    t.wait_ge(ss.h, t_xqT[i])
                    if i == 0 and jn == 0 and wq_wait is not None:
                        t.wait_ge(sin.h, wq_wait)
                    if tws is not None:
                        t.wait_ge(ss.h, tws)
                    if twh is not None:
                        t.wait_ge(sv.h, twh)
                    last = None
                    for kc in range(NKD):
                        t.matmul(self.psg[pb][:, :w],
                                 self.xqT[i % 3][:, kc * 128:(kc + 1) * 128],
                                 self.wgT[:, kc * FI + jn * 512: kc * FI + jn * 512 + w],
                                 start=(kc == 0), stop=(kc == NKD - 1))
                        last = t.matmul(self.psu[pb][:, :w],
                                        self.xqT[i % 3][:, kc * 128:(kc + 1) * 128],
                                        self.wuT[:, kc * FI + jn * 512: kc * FI + jn * 512 + w],
                                        start=(kc == 0), stop=(kc == NKD - 1))
                    last.then_inc(st.h, 1)
                self.E("tensor", tgu)
                t_gu = st.inc()
                def se(s, jn=jn, w=w, pb=pb, t=t_gu):
                    s.wait_ge(st.h, t)
                    s.activation(out=self.sg_sb[:, jn * 512:jn * 512 + w],
                                 in_=self.psg[pb][:, :w], func=ACT.Silu).then_inc(ss.h, 1)
                self.E("scalar", se)
                silu_done[jn] = ss.inc()
                def vh(v, jn=jn, w=w, pb=pb, t=t_gu, tsil=silu_done[jn]):
                    v.wait_ge(st.h, t)
                    v.wait_ge(ss.h, tsil)
                    sl = slice(jn * 512, jn * 512 + w)
                    v.tensor_tensor(out=self.sg_sb[:, sl], in0=self.sg_sb[:, sl],
                                    in1=self.psu[pb][:, :w], op=ALU.mult).then_inc(sv.h, 1)
                self.E("vector", vh)
                hmul_done[jn] = sv.inc()
                self.emit_quant(self.sg_sb[:, jn * 512:jn * 512 + w],
                                self.hq[:, jn * 512:jn * 512 + w], w)
                def vq(v):
                    v.engine_nop().then_inc(sv.h, 1)
                self.E("vector", vq)
                hq_done[jn] = sv.inc()

            def emit_dn(jn):
                w = min(512, FI - jn * 512)
                nb = w // 128
                c0 = jn * 4
                srcs = [self.hq[:, (c0 + j) * 128:(c0 + j + 1) * 128] for j in range(nb)]
                dst = self.hqT[:, c0 * 128:(c0 + nb) * 128]
                t_hqT = self.emit_transpose_batch(srcs, dst,
                                                  twaits=[(sv.h, hq_done[jn]),
                                                          (ss.h, ss.c)])
                def td(t, jn=jn, nb=nb, c0=c0, t_hqT=t_hqT,
                       twp=(resid_done.get(i - 1) if jn == 0 else None)):
                    t.wait_ge(ss.h, t_hqT)
                    if twp is not None:
                        t.wait_ge(sv.h, twp)
                    last = None
                    for kc in range(c0, c0 + nb):
                        for j2 in range(2):
                            last = t.matmul(self.psd[j2][:, :512],
                                            self.hqT[:, kc * 128:(kc + 1) * 128],
                                            self.wdT[:, kc * D + j2 * 512: kc * D + j2 * 512 + 512],
                                            start=(kc == 0),
                                            stop=(kc == NKI - 1))
                    if jn == NJG - 1:
                        last.then_inc(st.h, 1)
                self.E("tensor", td)
                if jn == NJG - 1:
                    return st.inc()
                return None

            t_d = None
            for jn in range(NJG):
                emit_gu(jn)
                if jn == 1 and mid_hook is not None:
                    mid_hook()
                if jn == 3:
                    emit_xq_tp(i + 2)
                if jn >= 1:
                    t_d = emit_dn(jn - 1)
            t_d = emit_dn(NJG - 1)
            silu_hist[i] = silu_done
            hmul_hist[i] = hmul_done
            def vr(v, i=i, b=b, t=t_d, tw_out=out_done.get(i - 1)):
                v.wait_ge(st.h, t)
                if tw_out is not None:
                    v.wait_ge(sgp.h, tw_out)
                last = None
                for jn in range(2):
                    sl = slice(jn * 512, jn * 512 + 512)
                    last = v.scalar_tensor_tensor(out=self.x1t[:, sl], in0=self.psd[jn][:, :512],
                                                  scalar=0.5, in1=self.xt3[bx][:, sl],
                                                  op0=ALU.mult, op1=ALU.add)
                last.then_inc(sv.h, 1)
            self.E("vector", vr)
            resid_done[i] = sv.inc()
            def stx(g, i=i, t=resid_done[i]):
                g.wait_ge(sv.h, t)
                self.dma(g, sgp, dst_hbm[i * 128:(i + 1) * 128, :], self.x1t)
            self.E("gpsimd", stx)
            out_done[i] = sgp.inc(16)

        stage1(0)
        emit_xq_tp(0)
        stage1(1)
        emit_xq_tp(1)
        for i in range(NT):
            hook = (lambda j=i + 2: stage1(j)) if i + 2 < NT else None
            stage2(i, mid_hook=hook)

    def phase_b1(self):
        sin, sv, st, ss, sgp = self.sin, self.sv, self.st, self.ss, self.sgp
        def gw(g):
            self.dma(g, sgp, self.Wk_sb.rearrange("p (kc n) -> p kc n", n=D), self.Wk_in.rearrange("(kc p) n -> p kc n", p=128))
            self.dma(g, sgp, self.Wv_sb.rearrange("p (kc n) -> p kc n", n=D), self.Wv_in.rearrange("(kc p) n -> p kc n", p=128))
        self.E("gpsimd", gw)
        t_w = sgp.inc(32)
        prev_evac = {}
        for i in range(NT):
            b = i % 2
            def ld(sy, i=i, b=b, tw=prev_evac.get(i - 2)):
                if tw is not None:
                    sy.wait_ge(ss.h, tw)
                self.dma(sy, sin, self.xt[b], self.x1_hbm[i * 128:(i + 1) * 128, :])
            self.E("sync", ld)
            t_x = sin.inc(16)
            def vw(v, t=t_x):
                v.wait_ge(sin.h, t)
            self.E("vector", vw)
            self.emit_ln(self.xt[b], self.xln_bf, s_extra_waits=[(sin.h, t_x)],
                         sqscr=self.lnsq)
            def vq(v):
                v.engine_nop().then_inc(sv.h, 1)
            self.E("vector", vq)
            t_ln = sv.inc()
            srcs = [self.xln_bf[:, j * 128:(j + 1) * 128] for j in range(NKD)]
            tws = [(sv.h, t_ln), (ss.h, ss.c)]
            if i == 0:
                tws.append((sgp.h, t_w))
            t_xlnT = self.emit_transpose_batch(srcs, self.xlnT, twaits=tws)
            for half in range(2):
                pbank = self.psg[half]
                def tk(t, half=half, t_xlnT=t_xlnT, pbank=pbank,
                       twp=prev_evac.get(i - 1)):
                    t.wait_ge(ss.h, t_xlnT)
                    if twp is not None and half == 0:
                        t.wait_ge(ss.h, twp)
                    last = None
                    for kc in range(NKD):
                        for nl in range(4):
                            nc_ = half * 4 + nl
                            last = t.matmul(pbank[:, nl * 128:(nl + 1) * 128],
                                            self.Wk_sb[:, kc * D + nc_ * 128: kc * D + nc_ * 128 + 128],
                                            self.xlnT[:, kc * 128:(kc + 1) * 128],
                                            start=(kc == 0 and nl == 0),
                                            stop=(kc == NKD - 1 and nl == 3))
                    last.then_inc(st.h, 1)
                self.E("tensor", tk)
                t_k = st.inc()
                def sk(s, i=i, half=half, pbank=pbank, t=t_k):
                    s.wait_ge(st.h, t)
                    last = None
                    for nl in range(4):
                        nc_ = half * 4 + nl
                        last = s.activation(
                            out=self.k_sb[:, nc_ * S + i * 128: nc_ * S + i * 128 + 128],
                            in_=pbank[:, nl * 128:(nl + 1) * 128],
                            func=ACT.Relu, bias=self.bkp[:, nc_:nc_ + 1])
                    last.then_inc(ss.h, 1)
                self.E("scalar", sk)
                ss.inc()
            def tv(t, t_xlnT=t_xlnT, twp=prev_evac.get(i - 1)):
                if twp is not None:
                    t.wait_ge(ss.h, twp)
                for kc in range(NKD):
                    for jh in range(2):
                        t.matmul(self.psu[jh][:, :512],
                                 self.xlnT[:, kc * 128:(kc + 1) * 128],
                                 self.Wv_sb[:, kc * D + jh * 512: kc * D + jh * 512 + 512],
                                 start=(kc == 0), stop=False)
                last = None
                for jh in range(2):
                    last = t.matmul(self.psu[jh][:, :512], self.onesb,
                                    self.bvb[:, jh * 512:jh * 512 + 512],
                                    start=False, stop=True)
                last.then_inc(st.h, 1)
            self.E("tensor", tv)
            t_v = st.inc()
            def sv_(s, i=i, t=t_v):
                s.wait_ge(st.h, t)
                last = None
                for jh in range(2):
                    last = s.activation(
                        out=self.v_sb[:, i * D + jh * 512: i * D + jh * 512 + 512],
                        in_=self.psu[jh][:, :512], func=ACT.Relu)
                last.then_inc(ss.h, 1)
            self.E("scalar", sv_)
            prev_evac[i] = ss.inc()

    def phase_b2(self):
        sin, sv, st, ss, sgp = self.sin, self.sv, self.st, self.ss, self.sgp
        def gw(g):
            self.dma(g, sgp, self.Wq_sb.rearrange("p (kc n) -> p kc n", n=D),
                     self.Wq_in.rearrange("(kc p) n -> p kc n", p=128))
            self.dma(g, sgp, self.Wo_sb.rearrange("p (kc n) -> p kc n", n=D),
                     self.Wo_in.rearrange("(kc p) n -> p kc n", p=128))
        self.E("gpsimd", gw)
        t_w = sgp.inc(32)
        x2_out = {}
        t_qdone = {}
        t_x2v = {}
        t_yv = {}
        t_xlnTd = {}
        pend_ln = {}
        t_xd = {}
        pv_hist = {}

        def emit_xln_tp(j):
            if j not in pend_ln or j in t_xlnTd:
                return
            srcs = [self.xln_bf[:, c * 128:(c + 1) * 128] for c in range(NKD)]
            tws = [(sv.h, pend_ln[j]), (ss.h, ss.c)]
            if j == 0:
                tws.append((sgp.h, t_w))
            t_xlnTd[j] = self.emit_transpose_batch(srcs, self.xlnT, twaits=tws)

        def s_ld(i):
            b = i % 2
            tw = (t_w + 16 * (i - 1)) if i >= 2 else None
            def ld(sy, i=i, b=b, tw=tw):
                if tw is not None:
                    sy.wait_ge(sgp.h, tw)
                self.dma(sy, sin, self.xt[b], self.x1_hbm[i * 128:(i + 1) * 128, :])
            self.E("sync", ld)
            t_xd[i] = sin.inc(16)

        def qproj(i):
            b = i % 2
            for half in range(2):
                pbank = self.psg[half]
                def tq(t, half=half, b=b, t_xlnT=t_xlnTd[i], pbank=pbank,
                       twp=t_qdone.get(i - 1)):
                    t.wait_ge(ss.h, t_xlnT)
                    if twp is not None and half == 0:
                        t.wait_ge(ss.h, twp)
                    last = None
                    for kc in range(NKD):
                        for nl in range(4):
                            nc_ = half * 4 + nl
                            last = t.matmul(pbank[:, nl * 128:(nl + 1) * 128],
                                            self.Wq_sb[:, kc * D + nc_ * 128: kc * D + nc_ * 128 + 128],
                                            self.xlnT[:, kc * 128:(kc + 1) * 128],
                                            start=(kc == 0 and nl == 0),
                                            stop=(kc == NKD - 1 and nl == 3))
                    last.then_inc(st.h, 1)
                self.E("tensor", tq)
                t_q = st.inc()
                def sq(s, half=half, b=b, pbank=pbank, t=t_q):
                    s.wait_ge(st.h, t)
                    last = None
                    for nl in range(4):
                        nc_ = half * 4 + nl
                        last = s.activation(out=self.q_fm[:, nc_ * 128:(nc_ + 1) * 128],
                                            in_=pbank[:, nl * 128:(nl + 1) * 128],
                                            func=ACT.Relu, scale=1.0 / 32.0,
                                            bias=self.bqp32[:, nc_:nc_ + 1])
                    last.then_inc(ss.h, 1)
                self.E("scalar", sq)
                t_qdone[i] = ss.inc()

        qk_evacs = {}

        def emit_qk(i, g0_start, extra_evac_waits=(), extra_t_waits=()):
            b = i % 2
            W = (i + 1) * 128
            ngrp = (W + 511) // 512
            qk_evac = qk_evacs.setdefault(i, {})
            for g0 in range(g0_start, ngrp, 2):
                gpair = tuple(g for g in (g0, g0 + 1) if g < ngrp)
                if gpair[0] in qk_evac:
                    continue
                def tqk(t, gpair=gpair, W=W, tq_=t_qdone[i],
                        twp=qk_evac.get(g0 - 2),
                        twx=(t_x2v.get(i - 1) if g0 == 0 else None),
                        etw=(tuple(extra_t_waits) if g0 == g0_start else ())):
                    t.wait_ge(ss.h, tq_)
                    for h, c in etw:
                        t.wait_ge(h, c)
                    if twx is not None:
                        t.wait_ge(sv.h, twx)
                    if twp is not None:
                        t.wait_ge(sv.h, twp)
                    last = None
                    for kc in range(NKD):
                        for g_ in gpair:
                            wg_ = min(512, W - g_ * 512)
                            last = t.matmul(self.psu[g_ % 2][:, :wg_],
                                            self.q_fm[:, kc * 128:(kc + 1) * 128],
                                            self.k_sb[:, kc * S + g_ * 512: kc * S + g_ * 512 + wg_],
                                            start=(kc == 0), stop=(kc == NKD - 1))
                    last.then_inc(st.h, 1)
                self.E("tensor", tqk)
                t_qk = st.inc()
                def vqk(v, gpair=gpair, W=W, t=t_qk, ew=tuple(extra_evac_waits)):
                    v.wait_ge(st.h, t)
                    for h, c in ew:
                        v.wait_ge(h, c)
                    last = None
                    for g_ in gpair:
                        wg_ = min(512, W - g_ * 512)
                        last = v.tensor_copy(out=self.scores[:, g_ * 512:g_ * 512 + wg_],
                                             in_=self.psu[g_ % 2][:, :wg_])
                    last.then_inc(sv.h, 1)
                self.E("vector", vqk)
                t_e = sv.inc()
                for g_ in gpair:
                    qk_evac[g_] = t_e

        def s2(i):
            b = i % 2
            W = (i + 1) * 128
            ngrp = (W + 511) // 512
            emit_qk(i, 2)
            qk_evac = qk_evacs[i]
            def vmask(v, i=i, W=W):
                v.tensor_tensor(out=self.scores[:, i * 128:W], in0=self.scores[:, i * 128:W],
                                in1=self.causal, op=ALU.subtract)
                v.tensor_reduce(out=self.negmax, in_=self.scores[:, 0:W], axis=AX.X,
                                op=ALU.max, negate=True)
                v.engine_nop().then_inc(sv.h, 1)
            self.E("vector", vmask)
            t_mask = sv.inc()
            def sexp(s, W=W, t=t_mask):
                s.wait_ge(sv.h, t)
                s.activation(out=self.attn_bf[:, 0:W], in_=self.scores[:, 0:W],
                             func=ACT.Exp, bias=self.negmax,
                             accum_out=self.rowsum).then_inc(ss.h, 1)
            self.E("scalar", sexp)
            t_exp = ss.inc()
            t_sqv = None
            if i + 1 < NT:
                pw = [(sin.h, t_xd[i + 1])]
                if pv_hist.get(i - 1) is not None:
                    pw.append((st.h, pv_hist[i - 1]))
                t_sqv = self.emit_ln_stats(self.xt[(i + 1) % 2], self.attnT,
                                           pre_waits=pw)
                self.emit_ln_norm(self.xt[(i + 1) % 2], self.xln_bf, t_sqv,
                                  pre_waits=[(ss.h, t_xlnTd[i])] if i in t_xlnTd else [])
                def vqn(v):
                    v.engine_nop().then_inc(sv.h, 1)
                self.E("vector", vqn)
                pend_ln[i + 1] = sv.inc()
            def vrcp(v, t=t_exp):
                v.wait_ge(ss.h, t)
                v.reciprocal(out=self.rcpsum, in_=self.rowsum).then_inc(sv.h, 1)
            self.E("vector", vrcp)
            t_rcp = sv.inc()
            nkt = i + 1
            first = True
            pv_prev = None
            for b0 in range(0, nkt, 8):
                nb = min(8, nkt - b0)
                srcs = [self.attn_bf[:, (b0 + j) * 128:(b0 + j + 1) * 128] for j in range(nb)]
                sws = [] if pv_prev is None else [(st.h, pv_prev)]
                t_aT = self.emit_transpose_batch(srcs, self.attnT[:, :nb * 128],
                                                 twaits=[(ss.h, max(t_exp, ss.c)),
                                                         (sv.h, sv.c)],
                                                 swaits=sws)
                def tpv(t, b0=b0, nb=nb, t_aT=t_aT, first=first,
                        last_batch=(b0 + nb == nkt),
                        twy=(t_yv.get(i - 1) if first else None)):
                    t.wait_ge(ss.h, t_aT)
                    if twy is not None:
                        t.wait_ge(sv.h, twy)
                    last = None
                    for j in range(nb):
                        for jh in range(2):
                            last = t.matmul(self.psd[jh][:, :512],
                                            self.attnT[:, j * 128:(j + 1) * 128],
                                            self.v_sb[:, (b0 + j) * D + jh * 512:
                                                      (b0 + j) * D + jh * 512 + 512],
                                            start=(first and j == 0),
                                            stop=(last_batch and j == nb - 1))
                    last.then_inc(st.h, 1)
                self.E("tensor", tpv)
                pv_prev = st.inc()
                first = False
            pv_hist[i] = pv_prev
            if i + 1 < NT:
                emit_xln_tp(i + 1)
                qproj(i + 1)
                own = qk_evacs.get(i, {})
                t_own = max(own.values()) if own else 0
                emit_qk(i + 1, 0, extra_evac_waits=[(ss.h, t_exp)],
                        extra_t_waits=([(sv.h, t_own)] if t_own else []))
            def vy(v, t=pv_prev, trc=t_rcp):
                v.wait_ge(st.h, t)
                v.wait_ge(sv.h, trc)
                self.vgap(v)
                self.vgap(v)
                for jh in range(2):
                    v.tensor_scalar_mul(out=self.y_bf[:, jh * 512:(jh + 1) * 512],
                                        in0=self.psd[jh][:, :512], scalar1=self.rcpsum)
                v.engine_nop().then_inc(sv.h, 1)
            self.E("vector", vy)
            t_y = sv.inc()
            t_yv[i] = t_y
            srcs = [self.y_bf[:, j * 128:(j + 1) * 128] for j in range(NKD)]
            t_yT = self.emit_transpose_batch(srcs, self.yT,
                                             twaits=[(sv.h, t_y), (ss.h, ss.c)])
            hoisted = qk_evacs.get(i + 1, {})
            t_hoist = max(hoisted.values()) if hoisted else None
            def to(t, t_yT=t_yT, th=t_hoist):
                t.wait_ge(ss.h, t_yT)
                if th is not None:
                    t.wait_ge(sv.h, th)
                for kc in range(NKD):
                    for jh in range(2):
                        t.matmul(self.psu[jh][:, :512],
                                 self.yT[:, kc * 128:(kc + 1) * 128],
                                 self.Wo_sb[:, kc * D + jh * 512: kc * D + jh * 512 + 512],
                                 start=(kc == 0), stop=False)
                last = None
                for jh in range(2):
                    last = t.matmul(self.psu[jh][:, :512], self.onesb,
                                    self.bob[:, jh * 512:jh * 512 + 512],
                                    start=False, stop=True)
                last.then_inc(st.h, 1)
            self.E("tensor", to)
            t_o = st.inc()
            def vo(v, b=b, t=t_o):
                v.wait_ge(st.h, t)
                last = None
                for jh in range(2):
                    sl = slice(jh * 512, (jh + 1) * 512)
                    last = v.scalar_tensor_tensor(out=self.xt[b][:, sl],
                                                  in0=self.psu[jh][:, :512],
                                                  scalar=0.0, in1=self.xt[b][:, sl],
                                                  op0=ALU.max, op1=ALU.add)
                last.then_inc(sv.h, 1)
            self.E("vector", vo)
            t_x2 = sv.inc()
            t_x2v[i] = t_x2
            def stx(g, i=i, b=b, t=t_x2):
                g.wait_ge(sv.h, t)
                self.dma(g, sgp, self.x2_hbm[i * 128:(i + 1) * 128, :], self.xt[b])
            self.E("gpsimd", stx)
            x2_out[i] = sgp.inc(16)
            if i + 2 < NT:
                s_ld(i + 2)

        s_ld(0)
        s_ld(1)
        t_sqv0 = self.emit_ln_stats(self.xt[0], self.attnT,
                                    pre_waits=[(sin.h, t_xd[0])])
        self.emit_ln_norm(self.xt[0], self.xln_bf, t_sqv0)
        def vq0(v):
            v.engine_nop().then_inc(sv.h, 1)
        self.E("vector", vq0)
        pend_ln[0] = sv.inc()
        emit_xln_tp(0)
        qproj(0)
        emit_qk(0, 0)
        for i in range(NT):
            s2(i)


def _build_inputs(x_shard, ins):
    causal = (np.arange(128)[None, :] > np.arange(128)[:, None]).astype(np.float32) * 10000.0
    m = {
        "wg_in": ins["L0_ffn_w_gate"],
        "wu_in": ins["L0_ffn_w_up"],
        "wd_in": ins["L0_ffn_w_down"],
        "Wq_in": ins["L0_attn_Wq"],
        "Wk_in": ins["L0_attn_Wk"],
        "Wv_in": ins["L0_attn_Wv"],
        "Wo_in": ins["L0_attn_Wo"],
        "bq_in": ins["L0_attn_bq"],
        "bk_in": ins["L0_attn_bk"],
        "bv_in": ins["L0_attn_bv"],
        "bo_in": ins["L0_attn_bo"],
        "ident_in": np.eye(128, dtype=np.float32),
        "causal_in": causal,
        "ones_in": np.ones((1, 128), dtype=np.float32),
    }
    m["x_in"] = x_shard
    return m


def kernel(L0_attn_x, L0_attn_Wq, L0_attn_bq, L0_attn_Wk, L0_attn_bk,
           L0_attn_Wv, L0_attn_bv, L0_attn_Wo, L0_attn_bo,
           L0_ffn_w_gate, L0_ffn_w_up, L0_ffn_w_down):
    from concourse.bass_utils import run_bass_kernel_spmd
    ins = {k: np.ascontiguousarray(np.asarray(v, dtype=np.float32)) for k, v in {
        "L0_ffn_w_gate": L0_ffn_w_gate, "L0_ffn_w_up": L0_ffn_w_up,
        "L0_ffn_w_down": L0_ffn_w_down, "L0_attn_Wq": L0_attn_Wq,
        "L0_attn_Wk": L0_attn_Wk, "L0_attn_Wv": L0_attn_Wv,
        "L0_attn_Wo": L0_attn_Wo, "L0_attn_bq": L0_attn_bq,
        "L0_attn_bk": L0_attn_bk, "L0_attn_bv": L0_attn_bv,
        "L0_attn_bo": L0_attn_bo}.items()}
    x = np.ascontiguousarray(np.asarray(L0_attn_x, dtype=np.float32))
    kb = KB(dev_outputs=False)
    nc = kb.build(phases=("P", "A", "B1", "B2", "C"))
    mybir.codegen_inst_isa_subclasses(nc)
    in_maps = [_build_inputs(np.ascontiguousarray(x[c]), ins) for c in range(8)]
    res = run_bass_kernel_spmd(nc, in_maps, core_ids=list(range(8)))
    return np.stack([np.asarray(r["out"], dtype=np.float32) for r in res.results], axis=0)
